# version 03
# baseline (speedup 1.0000x reference)
import numpy as np
import ml_dtypes
from contextlib import ExitStack

import concourse.bass as bass
import concourse.mybir as mybir
from concourse.bass_utils import run_bass_kernel_spmd

F32 = mybir.dt.float32
BF16 = mybir.dt.bfloat16
I32 = mybir.dt.int32
AF = mybir.ActivationFunctionType
OP = mybir.AluOpType
AX = mybir.AxisListType
f32 = np.float32

P = 128
S = 2048
NT = 16
DM = 1024
NC = 8
II = 4096
NI = 32
TB = 512
NB = 4
NPAIR = 8
EPS = 1e-5


_VOPS = ("tensor_scalar", "tensor_tensor", "tensor_add", "tensor_mul",
         "tensor_copy", "reduce_sum", "reduce_max", "reciprocal", "memset",
         "scalar_tensor_tensor", "engine_nop")


class _PubInstr:

    def __init__(self, eng, instr):
        self._eng = eng
        self._i = instr

    def then_inc(self, sem, n):
        d = self._eng.drain()
        d.then_inc(sem, n)
        return d


class _DV:

    def __init__(self, eng):
        self._e = eng

    def __getattr__(self, name):
        attr = getattr(self._e, name)
        if name in _VOPS:
            e = self._e

            def wrapped(*a, **k):
                e.drain()
                return _PubInstr(e, attr(*a, **k))
            return wrapped
        return attr


_SOPS = ("activation", "copy")


class _DS:

    def __init__(self, eng):
        self._e = eng

    def __getattr__(self, name):
        attr = getattr(self._e, name)
        if name in _SOPS:
            e = self._e

            def wrapped(*a, **k):
                return _PubInstr(e, attr(*a, **k))
            return wrapped
        return attr


def build_core():
    nc = bass.Bass("TRN2", target_bir_lowering=False, detect_race_conditions=False)
    es = ExitStack()
    C = {"sdA": 0, "sdG": 0, "sT": 0, "sV": 0, "sS": 0}

    dx = nc.dram_tensor("x", [S, DM], F32, kind="ExternalInput")
    dWq = nc.dram_tensor("Wq", [DM, DM], F32, kind="ExternalInput")
    dWk = nc.dram_tensor("Wk", [DM, DM], F32, kind="ExternalInput")
    dWv = nc.dram_tensor("Wv", [DM, DM], F32, kind="ExternalInput")
    dWo = nc.dram_tensor("Wo", [DM, DM], F32, kind="ExternalInput")
    dbq = nc.dram_tensor("bq", [1, DM], F32, kind="ExternalInput")
    dbk = nc.dram_tensor("bk", [1, DM], F32, kind="ExternalInput")
    dbv = nc.dram_tensor("bv", [1, DM], F32, kind="ExternalInput")
    dbo = nc.dram_tensor("bo", [1, DM], F32, kind="ExternalInput")
    dwg = nc.dram_tensor("wg", [II, DM], F32, kind="ExternalInput")
    dwu = nc.dram_tensor("wu", [II, DM], F32, kind="ExternalInput")
    dwd = nc.dram_tensor("wd", [DM, II], F32, kind="ExternalInput")
    didb = nc.dram_tensor("idb", [P, P], BF16, kind="ExternalInput")
    didf = nc.dram_tensor("idf", [P, P], F32, kind="ExternalInput")
    dtri = nc.dram_tensor("tri", [P, P], BF16, kind="ExternalInput")
    dones = nc.dram_tensor("onesf", [1, P], F32, kind="ExternalInput")
    donesb = nc.dram_tensor("onesb", [1, P], BF16, kind="ExternalInput")
    dout = nc.dram_tensor("out", [S, DM], F32, kind="ExternalOutput")
    dswg = nc.dram_tensor("swg", [II * NC, P], BF16, kind="Internal")
    dswu = nc.dram_tensor("swu", [II * NC, P], BF16, kind="Internal")
    dswd = nc.dram_tensor("swd", [II * NC, P], BF16, kind="Internal")

    def sb(name, shape, dt):
        return es.enter_context(nc.sbuf_tensor(name, shape, dt))

    def ps_(name, shape, dt):
        return es.enter_context(nc.psum_tensor(name, shape, dt))

    identb = sb("identb", [P, P], BF16)
    identf = sb("identf", [P, P], F32)
    tri = sb("tri_sb", [P, P], BF16)
    onesr = sb("onesr", [P, P], F32)
    onesbb = sb("onesbb", [P, 64], BF16)
    bcols = sb("bcols", [P, 4 * NC], F32)
    mw = sb("mw", [1, 8], F32)
    k1bc = sb("k1bc", [P, 1], F32)
    k2bc = sb("k2bc", [P, 1], F32)
    onescol = sb("onescol", [P, 1], F32)

    pg0 = ps_("pg0", [P, 512], F32)
    pg1 = ps_("pg1", [P, 512], F32)
    pu0 = ps_("pu0", [P, 512], F32)
    pu1 = ps_("pu1", [P, 512], F32)
    pd0 = ps_("pd0", [P, 512], F32)
    pd1 = ps_("pd1", [P, 512], F32)
    psmall = ps_("psmall", [P, 512], F32)
    pt = ps_("pt", [P, 512], BF16)

    sdA = es.enter_context(nc.semaphore("sdA"))
    sdG = es.enter_context(nc.semaphore("sdG"))
    sT = es.enter_context(nc.semaphore("sT"))
    sV = es.enter_context(nc.semaphore("sV"))
    sS = es.enter_context(nc.semaphore("sS"))
    SEM = {"sdA": sdA, "sdG": sdG, "sT": sT, "sV": sV, "sS": sS}

    def inc(instr, s, n=1):
        C[s] += n
        instr.then_inc(SEM[s], n)
        return C[s]

    class DmaQ:

        def __init__(self, name, n=4):
            self.sems = [es.enter_context(nc.semaphore(f"{name}{i}")) for i in range(n)]
            self.tot = [0] * n
            self.i = 0

        def issue(self, eng, fns):
            j = self.i % len(self.sems)
            self.i += 1
            eng.wait_ge(self.sems[j], self.tot[j])
            for fn in fns:
                instr = fn()
                self.tot[j] += 16
                instr.then_inc(self.sems[j], 16)
            return (j, self.tot[j])

        def wait(self, eng, ticket):
            eng.wait_ge(self.sems[ticket[0]], ticket[1])

        def wait_all(self, eng):
            for j, s in enumerate(self.sems):
                eng.wait_ge(s, self.tot[j])

    QA = DmaQ("qa")
    QG = DmaQ("qg")

    free = {"pt": None, "psmall": None}

    def wait_free(eng, key):
        if free[key] is not None:
            eng.wait_ge(SEM[free[key][0]], free[key][1])

    blk = es.enter_context(nc.Block())

    _real_vec_dec = blk.vector
    _real_sca_dec = blk.scalar

    def vecblk(fn):
        return _real_vec_dec(lambda vector: fn(_DV(vector)))
    blk_vector = vecblk

    def scablk(fn):
        return _real_sca_dec(lambda scalar: fn(_DS(scalar)))
    blk_scalar = scablk

    tk_consts = [None]

    @blk.sync
    def _(sync):
        fns = [lambda d=dst, s=srcd: sync.dma_start(d[:], s.ap())
               for dst, srcd in ((identb, didb), (identf, didf), (tri, dtri))]
        fns.append(lambda: sync.dma_start(onesr[0:1, :], dones.ap()))
        fns.append(lambda: sync.dma_start(onesbb[64:65, :],
                                          bass.AP(donesb, 0, [[1, 1], [1, 64]])))
        for i4, db in enumerate((dbq, dbk, dbv, dbo)):
            fns.append(lambda i4=i4, db=db: sync.dma_start(
                bass.AP(bcols, 8 * i4, [[4 * NC, P], [1, NC]]),
                bass.AP(db, 0, [[1, P], [128, NC]]),
                allow_slow_non_contiguous=True))
        tk_consts[0] = QA.issue(sync, fns)

    @blk_vector
    def _(vector):
        QA.wait(vector, tk_consts[0])
        inc(vector.memset(onescol[:], 1.0), "sV")

    @blk.tensor
    def _(tensor):
        QA.wait(tensor, tk_consts[0])

    @blk_scalar
    def _(scalar):
        QA.wait(scalar, tk_consts[0])

    def quant_weight(widx, dsrc, dscr, is_wd):
        with (
            nc.sbuf_tensor(f"W32_{widx}", [P, 32768], F32) as W32,
            nc.sbuf_tensor(f"sgA_{widx}", [P, 4096], F32) as sgA,
            nc.sbuf_tensor(f"sgB_{widx}", [P, 4096], F32) as sgB,
            nc.sbuf_tensor(f"tq_{widx}", [P, 8192], BF16) as tq,
            nc.sbuf_tensor(f"st2_{widx}", [P, 8192], BF16) as st2,
            nc.sbuf_tensor(f"prt_{widx}", [P, 10], F32) as prt,
            nc.sbuf_tensor(f"acc8_{widx}", [P, 8], F32) as acc8,
            nc.sbuf_tensor(f"sbc_{widx}", [P, 1], F32) as sbc,
        ):
            vw = C["sV"]
            sw = C["sS"]
            tw = C["sT"]
            w32_tk = []

            @blk.sync
            def _(sync):
                sync.wait_ge(sV, vw)
                sync.wait_ge(sS, sw)
                sync.wait_ge(sT, tw)
                QG.wait_all(sync)
                for j in range(4):
                    if is_wd:
                        fn = lambda j=j: sync.dma_start(
                            bass.AP(W32, j * 8192, [[32768, P], [4096, 2], [1, 4096]]),
                            bass.AP(dsrc, j * 2 * P * II, [[II, P], [P * II, 2], [1, II]]))
                    else:
                        fn = lambda j=j: sync.dma_start(
                            bass.AP(W32, j * 8192, [[32768, P], [1024, 8], [1, 1024]]),
                            bass.AP(dsrc, j * 8 * P * DM, [[DM, P], [P * DM, 8], [1, DM]]))
                    w32_tk.append(QA.issue(sync, [fn]))

            @blk_vector
            def _(vector):
                for j in range(4):
                    QA.wait(vector, w32_tk[j])
                    vector.reduce_sum(prt[:, 2 * j:2 * j + 1],
                                      W32[:, j * 8192:j * 8192 + 4096],
                                      AX.X, apply_absolute_value=True)
                    r = vector.reduce_sum(prt[:, 2 * j + 1:2 * j + 2],
                                          W32[:, j * 8192 + 4096:(j + 1) * 8192],
                                          AX.X, apply_absolute_value=True)
                    if j == 3:
                        inc(r, "sV")
            v_sum = C["sV"]

            @blk_scalar
            def _(scalar):
                scalar.wait_ge(sV, v_sum)
                inc(scalar.activation(acc8[:], prt[:, 0:8], AF.Identity,
                                      accum_out=prt[:, 8:9]), "sS")
            s_tot8 = C["sS"]

            @blk.tensor
            def _(tensor):
                tensor.wait_ge(sS, s_tot8)
                wait_free(tensor, "psmall")
                inc(tensor.matmul(psmall[0:1, 0:1], prt[:, 8:9], onescol[:, 0:1],
                                  start=True, stop=True), "sT")
            t_tot = C["sT"]

            @blk_vector
            def _(vector):
                vector.wait_ge(sT, t_tot)
                inc(vector.tensor_scalar(mw[0:1, widx:widx + 1], psmall[0:1, 0:1],
                                         1.0 / 4194304.0, EPS, OP.mult, OP.max), "sV")
            v_mw = C["sV"]

            @blk_scalar
            def _(scalar):
                scalar.wait_ge(sV, v_mw)
                inc(scalar.copy(mw[0:1, 6:7], mw[0:1, widx:widx + 1]), "sS")
            s_mw = C["sS"]

            @blk_vector
            def _(vector):
                vector.wait_ge(sS, s_mw)
                vector.reciprocal(mw[0:1, 7:8], mw[0:1, 6:7])
                vector.tensor_mul(mw[0:1, 5:6], mw[0:1, 6:7], mw[0:1, 7:8])
                vector.tensor_scalar(mw[0:1, 5:6], mw[0:1, 5:6], -1.0, 2.0,
                                     OP.mult, OP.add)
                inc(vector.tensor_mul(mw[0:1, 7:8], mw[0:1, 7:8], mw[0:1, 5:6]), "sV")
            v_m = C["sV"]

            @blk.tensor
            def _(tensor):
                tensor.wait_ge(sV, v_m)
                inc(tensor.matmul(psmall[:, 0:1], onesr[0:1, :], mw[0:1, 7:8],
                                  start=True, stop=True), "sT")
            t_bc = C["sT"]

            @blk_vector
            def _(vector):
                vector.wait_ge(sT, t_bc)
                inc(vector.tensor_copy(sbc[:], psmall[:, 0:1]), "sV")
            v_sbc = C["sV"]
            free["psmall"] = ("sV", v_sbc)

            tq_tdone = [None] * 8
            st2_gdone = [None] * 8
            for j in range(8):
                jb = j % 2

                @blk_scalar
                def _(scalar):
                    scalar.wait_ge(sV, v_sbc)
                    if j >= 1:
                        scalar.wait_ge(sV, C["sV"])
                    scalar.activation(sgA[:], W32[:, j * 4096:(j + 1) * 4096], AF.Sign,
                                      bias=f32(-0.5), scale=sbc[:, 0:1])
                    inc(scalar.activation(sgB[:], W32[:, j * 4096:(j + 1) * 4096], AF.Sign,
                                          bias=f32(0.5), scale=sbc[:, 0:1]), "sS")
                s_sg = C["sS"]

                @blk_vector
                def _(vector):
                    vector.wait_ge(sS, s_sg)
                    if j >= 2 and tq_tdone[j - 2] is not None:
                        vector.wait_ge(sT, tq_tdone[j - 2])
                    inc(vector.tensor_add(tq[:, jb * 4096:(jb + 1) * 4096], sgA[:], sgB[:]), "sV")
                v_tq = C["sV"]

                t_tr_base = C["sT"]

                @blk.tensor
                def _(tensor):
                    tensor.wait_ge(sV, v_tq)
                    for bat in range(8):
                        if bat == 0:
                            wait_free(tensor, "pt")
                        else:
                            tensor.wait_ge(sS, s_sg + bat)
                        last = None
                        for k in range(4):
                            bi = bat * 4 + k
                            last = tensor.transpose(
                                pt[:, k * 128:(k + 1) * 128],
                                tq[:, jb * 4096 + bi * 128:jb * 4096 + (bi + 1) * 128],
                                identb[:])
                        inc(last, "sT")
                tq_tdone[j] = C["sT"]

                @blk_scalar
                def _(scalar):
                    for bat in range(8):
                        scalar.wait_ge(sT, t_tr_base + bat + 1)
                        if j >= 2 and bat == 0:
                            QG.wait(scalar, st2_gdone[j - 2])
                        inc(scalar.copy(st2[:, jb * 4096 + bat * 512:jb * 4096 + (bat + 1) * 512],
                                        pt[:, 0:512]), "sS")
                s_evb = C["sS"]
                free["pt"] = ("sS", s_evb)

                @blk.gpsimd
                def _(g):
                    g.wait_ge(sS, s_evb)
                    st2_gdone[j] = QG.issue(g, [lambda: g.dma_start(
                        bass.AP(dscr, (j * 32) * 128 * P, [[P, P], [128 * P, 32], [1, P]]),
                        bass.AP(st2, jb * 4096, [[8192, P], [128, 32], [1, P]]))])

    quant_weight(0, dwg, dswg, False)
    quant_weight(1, dwu, dswu, False)
    quant_weight(2, dwd, dswd, True)

    x_sb = sb("x_sb", [P, NT * DM], F32)
    lnA = sb("lnA", [P, DM], F32)
    lnY = sb("lnY", [P, DM], F32)
    lnS = sb("lnS", [P, 16], F32)
    acol = sb("acol", [P, 4], F32)
    q_tok = sb("q_tok", [P, DM], BF16)
    m1sb = sb("m1sb", [P, 512], F32)
    sbrow = sb("sbrow", [1, TB], F32)
    ocols = sb("ocols", [P, 4], F32)

    @blk_vector
    def _(vector):
        vector.tensor_mul(mw[0:1, 3:4], mw[0:1, 0:1], mw[0:1, 0:1])
        vector.tensor_mul(mw[0:1, 3:4], mw[0:1, 3:4], mw[0:1, 1:2])
        vector.tensor_scalar(mw[0:1, 3:4], mw[0:1, 3:4], 1.0 / (127.0 ** 3 * 8.0), None, OP.mult)
        inc(vector.tensor_scalar(mw[0:1, 4:5], mw[0:1, 2:3], 0.5 / (127.0 * 2.0), None, OP.mult), "sV")
    v_k = C["sV"]

    @blk.tensor
    def _(tensor):
        tensor.wait_ge(sV, v_k)
        wait_free(tensor, "psmall")
        tensor.matmul(psmall[:, 0:1], onesr[0:1, :], mw[0:1, 3:4], start=True, stop=True)
        inc(tensor.matmul(psmall[:, 1:2], onesr[0:1, :], mw[0:1, 4:5], start=True, stop=True), "sT")
    t_k = C["sT"]

    @blk_vector
    def _(vector):
        vector.wait_ge(sT, t_k)
        vector.tensor_copy(k1bc[:], psmall[:, 0:1])
        inc(vector.tensor_copy(k2bc[:], psmall[:, 1:2]), "sV")
    free["psmall"] = ("sV", C["sV"])

    tk_x = [None]

    vq_done, sq_done, tq_done = C["sV"], C["sS"], C["sT"]

    @blk.sync
    def _(sync):
        sync.wait_ge(sV, vq_done)
        sync.wait_ge(sS, sq_done)
        sync.wait_ge(sT, tq_done)
        QG.wait_all(sync)
        tk_x[0] = QA.issue(sync, [lambda: sync.dma_start(
            bass.AP(x_sb, 0, [[NT * DM, P], [DM, NT], [1, DM]]),
            bass.AP(dx, 0, [[DM, P], [P * DM, NT], [1, DM]]))])

    def emit_ffn(call):
        with (
            nc.sbuf_tensor(f"wgr{call}", [P, 8 * 1024], BF16) as wgr,
            nc.sbuf_tensor(f"wur{call}", [P, 8 * 1024], BF16) as wur,
            nc.sbuf_tensor(f"wdr{call}", [P, 8 * 1024], BF16) as wdr,
            nc.sbuf_tensor(f"qT{call}", [P, NC * TB], BF16) as qT,
            nc.sbuf_tensor(f"harena{call}", [P, NI * TB], BF16) as harena,
            nc.sbuf_tensor(f"maxrun{call}", [P, TB], BF16) as maxrun,
            nc.sbuf_tensor(f"minrun{call}", [P, TB], BF16) as minrun,
            nc.sbuf_tensor(f"lnQi{call}", [P, DM], I32) as lnQi,
            nc.sbuf_tensor(f"lnCr{call}", [P, DM], F32) as lnCr,
            nc.sbuf_tensor(f"Rsb{call}", [P, 512], F32) as Rsb,
            nc.sbuf_tensor(f"R2sb{call}", [P, 512], F32) as R2sb,
        ):
            @blk.sync
            def _(sync):
                sync.wait_ge(sV, C["sV"])
                sync.wait_ge(sS, C["sS"])
                sync.wait_ge(sT, C["sT"])
                if call == 0:
                    QG.wait_all(sync)

            t_gu_hist = []
            t_dn_hist = []

            for b in range(NB):
                for j in range(4):
                    tile = 4 * b + j
                    xsl = x_sb[:, tile * DM:(tile + 1) * DM]

                    @blk_vector
                    def _(vector):
                        if call == 0 and b == 0 and j == 0:
                            QA.wait(vector, tk_x[0])
                        vector.reduce_sum(lnS[:, 0:1], xsl, AX.X)
                        inc(vector.tensor_scalar(lnS[:, 1:2], lnS[:, 0:1], -1.0 / DM,
                                                 None, OP.mult), "sV")
                    v_m = C["sV"]

                    @blk_scalar
                    def _(scalar):
                        scalar.wait_ge(sV, v_m)
                        inc(scalar.activation(lnA[:], xsl, AF.Square, bias=lnS[:, 1:2],
                                              accum_out=lnS[:, 2:3]), "sS")
                    s_sq = C["sS"]

                    @blk_vector
                    def _(vector):
                        vector.wait_ge(sS, s_sq)
                        inc(vector.tensor_scalar(lnS[:, 3:4], lnS[:, 2:3], 1.0 / DM, EPS,
                                                 OP.mult, OP.add), "sV")
                    v_t0 = C["sV"]

                    @blk_scalar
                    def _(scalar):
                        scalar.wait_ge(sV, v_t0)
                        inc(scalar.activation(lnS[:, 4:5], lnS[:, 3:4], AF.Sqrt), "sS")
                    s_rt = C["sS"]

                    @blk_vector
                    def _(vector):
                        vector.wait_ge(sS, s_rt)
                        vector.reciprocal(lnS[:, 5:6], lnS[:, 4:5])
                        for _it in range(2):
                            vector.tensor_mul(lnS[:, 6:7], lnS[:, 5:6], lnS[:, 5:6])
                            vector.tensor_mul(lnS[:, 6:7], lnS[:, 6:7], lnS[:, 3:4])
                            vector.tensor_scalar(lnS[:, 6:7], lnS[:, 6:7], -0.5, 1.5,
                                                 OP.mult, OP.add)
                            vector.tensor_mul(lnS[:, 5:6], lnS[:, 5:6], lnS[:, 6:7])
                        vector.tensor_mul(lnS[:, 7:8], lnS[:, 1:2], lnS[:, 5:6])
                        vector.tensor_scalar(lnY[:], xsl, lnS[:, 5:6], lnS[:, 7:8],
                                             OP.mult, OP.add)
                        vector.reduce_max(lnS[:, 8:9], lnY[:], AX.X, apply_absolute_value=True)
                        vector.tensor_scalar(lnS[:, 9:10], lnS[:, 8:9], EPS, None, OP.max)
                        vector.reciprocal(lnS[:, 10:11], lnS[:, 9:10])
                        vector.tensor_mul(lnS[:, 12:13], lnS[:, 9:10], lnS[:, 10:11])
                        vector.tensor_scalar(lnS[:, 12:13], lnS[:, 12:13], -1.0, 2.0,
                                             OP.mult, OP.add)
                        vector.tensor_mul(lnS[:, 10:11], lnS[:, 10:11], lnS[:, 12:13])
                        vector.tensor_scalar(lnS[:, 11:12], lnS[:, 10:11], 127.0, None, OP.mult)
                        vector.tensor_copy(acol[:, j:j + 1], lnS[:, 9:10])
                        vector.tensor_scalar(lnA[:], lnY[:], lnS[:, 11:12], 256.0, OP.mult, OP.add)
                        vector.tensor_copy(lnQi[:], lnA[:])
                        vector.scalar_tensor_tensor(lnCr[:], lnQi[:], -255.5, lnA[:],
                                                    OP.add, OP.is_le)
                        vector.wait_ge(sT, C["sT"])
                        inc(vector.scalar_tensor_tensor(q_tok[:], lnQi[:], -256.0, lnCr[:],
                                                        OP.add, OP.add), "sV")
                    v_q = C["sV"]

                    t_tr_base = C["sT"]
                    s_ev_base = C["sS"]

                    @blk.tensor
                    def _(tensor):
                        tensor.wait_ge(sV, v_q)
                        for half in range(2):
                            if half == 0:
                                wait_free(tensor, "pt")
                            else:
                                tensor.wait_ge(sS, s_ev_base + 1)
                            last = None
                            for k in range(4):
                                hc = half * 4 + k
                                last = tensor.transpose(pt[:, k * 128:(k + 1) * 128],
                                                        q_tok[:, hc * 128:(hc + 1) * 128],
                                                        identb[:])
                            inc(last, "sT")

                    @blk_scalar
                    def _(scalar):
                        for half in range(2):
                            scalar.wait_ge(sT, t_tr_base + half + 1)
                            hc0 = half * 4
                            inc(scalar.copy(
                                bass.AP(qT, hc0 * TB + j * 128, [[NC * TB, P], [TB, 4], [1, 128]]),
                                bass.AP(pt, 0, [[512, P], [128, 4], [1, 128]])), "sS")
                    free["pt"] = ("sS", C["sS"])

                v_lnq_done = C["sV"]
                s_qT_done = C["sS"]

                gw_tk = []
                t_gu_pred = C["sT"]

                @blk.sync
                def _(sync):
                    for g in range(16):
                        ic0 = 2 * g
                        if ic0 - 8 >= 0:
                            sync.wait_ge(sT, t_gu_pred + 2 * (ic0 - 8 + 2))
                        elif t_gu_hist:
                            sync.wait_ge(sT, t_gu_hist[-1] + 2 * (24 + ic0 + 2))
                        slot = ic0 % 8
                        fns = []
                        for wsb, dscr in ((wgr, dswg), (wur, dswu)):
                            fns.append(lambda wsb=wsb, dscr=dscr, slot=slot, ic0=ic0:
                                       sync.dma_start(
                                           bass.AP(wsb, slot * 1024,
                                                   [[8 * 1024, P], [P, 16], [1, P]]),
                                           bass.AP(dscr, ic0 * NC * P * P,
                                                   [[P, P], [P * P, 16], [1, P]])))
                        gw_tk.append(QA.issue(sync, fns))

                t_gu_base = C["sT"]
                assert t_gu_base == t_gu_pred
                t_gu_hist.append(t_gu_base)

                @blk.tensor
                def _(tensor):
                    tensor.wait_ge(sS, s_qT_done)
                    tensor.wait_ge(sV, v_lnq_done)
                    for ic in range(NI):
                        pg = pg0 if ic % 2 == 0 else pg1
                        pu = pu0 if ic % 2 == 0 else pu1
                        QA.wait(tensor, gw_tk[ic // 2])
                        if ic >= 2:
                            tensor.wait_ge(sS, s_qT_done + 2 * (ic - 1))
                            tensor.wait_ge(sV, v_lnq_done + (ic - 1))
                        slot = ic % 8
                        last = None
                        for hc in range(NC):
                            last = tensor.matmul(
                                pg[:], wgr[:, slot * 1024 + hc * 128:slot * 1024 + (hc + 1) * 128],
                                qT[:, hc * TB:(hc + 1) * TB],
                                start=(hc == 0), stop=(hc == NC - 1))
                        inc(last, "sT")
                        for hc in range(NC):
                            last = tensor.matmul(
                                pu[:], wur[:, slot * 1024 + hc * 128:slot * 1024 + (hc + 1) * 128],
                                qT[:, hc * TB:(hc + 1) * TB],
                                start=(hc == 0), stop=(hc == NC - 1))
                        inc(last, "sT")

                @blk_scalar
                def _(scalar):
                    for ic in range(NI):
                        pg = pg0 if ic % 2 == 0 else pg1
                        scalar.wait_ge(sT, t_gu_base + 2 * ic + 1)
                        scalar.activation(Rsb[:], pg[:], AF.Relu)
                        inc(scalar.activation(R2sb[:], Rsb[:], AF.Square), "sS", 2)

                @blk_vector
                def _(vector):
                    for ic in range(NI):
                        pu = pu0 if ic % 2 == 0 else pu1
                        vector.wait_ge(sT, t_gu_base + 2 * ic + 2)
                        vector.wait_ge(sS, s_qT_done + 2 * (ic + 1))
                        vector.tensor_mul(harena[:, ic * TB:(ic + 1) * TB], R2sb[:], pu[:])
                        hsl = harena[:, ic * TB:(ic + 1) * TB]
                        if ic == 0:
                            vector.tensor_tensor(maxrun[:], hsl, hsl, OP.max)
                            inc(vector.tensor_tensor(minrun[:], hsl, hsl, OP.min), "sV")
                        else:
                            vector.tensor_tensor(maxrun[:], hsl, maxrun[:], OP.max)
                            inc(vector.tensor_tensor(minrun[:], hsl, minrun[:], OP.min), "sV")
                v_B_done = C["sV"]

                @blk_vector
                def _(vector):
                    inc(vector.scalar_tensor_tensor(maxrun[:], minrun[:], -1.0, maxrun[:],
                                                    OP.mult, OP.max), "sV")
                v_B_done = C["sV"]

                @blk.tensor
                def _(tensor):
                    tensor.wait_ge(sV, v_B_done)
                    wait_free(tensor, "pt")
                    last = None
                    for k in range(4):
                        last = tensor.transpose(pt[:, k * 128:(k + 1) * 128],
                                                maxrun[:, k * 128:(k + 1) * 128], identb[:])
                    inc(last, "sT")
                t_mx = C["sT"]

                @blk_vector
                def _(vector):
                    vector.wait_ge(sT, t_mx)
                    for k in range(4):
                        vector.reduce_max(lnS[:, 12:13], pt[:, k * 128:(k + 1) * 128], AX.X)
                        vector.tensor_mul(lnS[:, 13:14], acol[:, k:k + 1], acol[:, k:k + 1])
                        vector.tensor_mul(lnS[:, 13:14], lnS[:, 13:14], acol[:, k:k + 1])
                        vector.tensor_scalar(lnS[:, 13:14], lnS[:, 13:14], k1bc[:, 0:1],
                                             None, OP.mult)
                        vector.tensor_mul(lnS[:, 14:15], lnS[:, 13:14], lnS[:, 12:13])
                        vector.tensor_scalar(lnS[:, 14:15], lnS[:, 14:15], EPS, None, OP.max)
                        vector.tensor_scalar(ocols[:, k:k + 1], lnS[:, 14:15], k2bc[:, 0:1],
                                             None, OP.mult)
                        vector.reciprocal(lnS[:, 15:16], lnS[:, 14:15])
                        vector.tensor_mul(lnS[:, 0:1], lnS[:, 14:15], lnS[:, 15:16])
                        vector.tensor_scalar(lnS[:, 0:1], lnS[:, 0:1], -1.0, 2.0,
                                             OP.mult, OP.add)
                        vector.tensor_mul(lnS[:, 15:16], lnS[:, 15:16], lnS[:, 0:1])
                        vector.tensor_scalar(lnS[:, 15:16], lnS[:, 15:16], 127.0, None, OP.mult)
                        vector.tensor_mul(lnS[:, 12:13], lnS[:, 13:14], lnS[:, 15:16])
                        vector.tensor_copy(acol[:, k:k + 1], lnS[:, 12:13])
                    inc(vector.engine_nop(), "sV")
                free["pt"] = ("sV", C["sV"])
                v_scol = C["sV"]

                @blk.tensor
                def _(tensor):
                    tensor.wait_ge(sV, v_scol)
                    wait_free(tensor, "psmall")
                    last = None
                    for k in range(4):
                        last = tensor.matmul(psmall[0:1, k * 128:(k + 1) * 128],
                                             acol[:, k:k + 1], identf[:], start=True, stop=True)
                    inc(last, "sT")
                t_row = C["sT"]

                @blk_vector
                def _(vector):
                    vector.wait_ge(sT, t_row)
                    inc(vector.tensor_copy(sbrow[0:1, :], psmall[0:1, 0:TB]), "sV")
                free["psmall"] = ("sV", C["sV"])
                v_row = C["sV"]

                @blk.tensor
                def _(tensor):
                    tensor.wait_ge(sV, v_row)
                    inc(tensor.matmul(pd0[:], onesr[0:1, :], sbrow[0:1, :],
                                      start=True, stop=True), "sT")
                t_srep = C["sT"]

                @blk_vector
                def _(vector):
                    vector.wait_ge(sT, t_srep)
                    last = None
                    for ic in range(NI):
                        vector.tensor_tensor(m1sb[:], harena[:, ic * TB:(ic + 1) * TB],
                                             pd0[:], OP.mult)
                        vector.tensor_scalar(lnQi[:, 0:TB], m1sb[:], 256.0, None, OP.add)
                        vector.scalar_tensor_tensor(lnCr[:, 0:TB], lnQi[:, 0:TB], -255.5,
                                                    m1sb[:], OP.add, OP.is_le)
                        last = vector.scalar_tensor_tensor(
                            harena[:, ic * TB:(ic + 1) * TB], lnQi[:, 0:TB], -256.0,
                            lnCr[:, 0:TB], OP.add, OP.add)
                    inc(last, "sV")
                v_ih = C["sV"]

                for sp in range(2):
                    wd_tk = []
                    t_dn_pred = C["sT"]

                    @blk.sync
                    def _(sync):
                        for icd in range(NI):
                            if icd - 8 >= 0:
                                sync.wait_ge(sT, t_dn_pred + icd - 8 + 1)
                            elif t_dn_hist:
                                sync.wait_ge(sT, t_dn_hist[-1] + 24 + icd + 1)
                            slot = icd % 8
                            wd_tk.append(QA.issue(sync, [
                                lambda icd=icd, slot=slot: sync.dma_start(
                                    bass.AP(wdr, slot * 1024,
                                            [[8 * 1024, P], [128, NC], [1, P]]),
                                    bass.AP(dswd, icd * P * P,
                                            [[P, P], [NI * P * P, NC], [1, P]]))]))

                    t_dn_base = C["sT"]
                    assert t_dn_base == t_dn_pred
                    t_dn_hist.append(t_dn_base)
                    v_prev_evac = C["sV"]

                    @blk.tensor
                    def _(tensor):
                        tensor.wait_ge(sV, v_prev_evac)
                        for ic in range(NI):
                            QA.wait(tensor, wd_tk[ic])
                            last = None
                            for tc2 in range(2):
                                tc = 2 * sp + tc2
                                for half in range(2):
                                    pp = (pg0, pg1, pu0, pu1)[tc2 * 2 + half]
                                    last = tensor.matmul(
                                        pp[:],
                                        harena[:, ic * TB + tc * 128:ic * TB + (tc + 1) * 128],
                                        wdr[:, (ic % 8) * 1024 + half * 512:(ic % 8) * 1024 + (half + 1) * 512],
                                        start=(ic == 0), stop=(ic == NI - 1))
                            inc(last, "sT")
                    t_dn_done = C["sT"]

                    @blk_vector
                    def _(vector):
                        vector.wait_ge(sT, t_dn_done)
                        last = None
                        for tc2 in range(2):
                            tc = 2 * sp + tc2
                            tile = 4 * b + tc
                            for half in range(2):
                                pp = (pg0, pg1, pu0, pu1)[tc2 * 2 + half]
                                xs2 = x_sb[:, tile * DM + half * 512:tile * DM + (half + 1) * 512]
                                last = vector.scalar_tensor_tensor(
                                    xs2, pp[:], ocols[:, tc:tc + 1], xs2, OP.mult, OP.add)
                        inc(last, "sV")

    def emit_attn():
        v_ffn_done = C["sV"]
        with (
            nc.sbuf_tensor("anorm", [P, NPAIR * S], BF16) as anorm,
            nc.sbuf_tensor("bvrep", [P, DM], F32) as bvrep,
            nc.sbuf_tensor("borep", [P, DM], F32) as borep,
        ):
            with (
                nc.sbuf_tensor("y2T", [P, NC * S], BF16) as y2T,
                nc.sbuf_tensor("vaug", [P, NT * 1040], BF16) as vaug,
            ):

                for tile in range(NT):
                    xsl = x_sb[:, tile * DM:(tile + 1) * DM]

                    @blk_vector
                    def _(vector):
                        if tile == 0:
                            vector.wait_ge(sV, v_ffn_done)
                        vector.reduce_sum(lnS[:, 0:1], xsl, AX.X)
                        inc(vector.tensor_scalar(lnS[:, 1:2], lnS[:, 0:1], -1.0 / DM,
                                                 None, OP.mult), "sV")
                    v_m = C["sV"]

                    @blk_scalar
                    def _(scalar):
                        scalar.wait_ge(sV, v_m)
                        inc(scalar.activation(lnA[:], xsl, AF.Square, bias=lnS[:, 1:2],
                                              accum_out=lnS[:, 2:3]), "sS")
                    s_sq = C["sS"]

                    @blk_vector
                    def _(vector):
                        vector.wait_ge(sS, s_sq)
                        inc(vector.tensor_scalar(lnS[:, 3:4], lnS[:, 2:3], 1.0 / DM, EPS,
                                                 OP.mult, OP.add), "sV")
                    v_t0 = C["sV"]

                    @blk_scalar
                    def _(scalar):
                        scalar.wait_ge(sV, v_t0)
                        inc(scalar.activation(lnS[:, 4:5], lnS[:, 3:4], AF.Sqrt), "sS")
                    s_rt = C["sS"]

                    @blk_vector
                    def _(vector):
                        vector.wait_ge(sS, s_rt)
                        vector.reciprocal(lnS[:, 5:6], lnS[:, 4:5])
                        for _it in range(2):
                            vector.tensor_mul(lnS[:, 6:7], lnS[:, 5:6], lnS[:, 5:6])
                            vector.tensor_mul(lnS[:, 6:7], lnS[:, 6:7], lnS[:, 3:4])
                            vector.tensor_scalar(lnS[:, 6:7], lnS[:, 6:7], -0.5, 1.5,
                                                 OP.mult, OP.add)
                            vector.tensor_mul(lnS[:, 5:6], lnS[:, 5:6], lnS[:, 6:7])
                        vector.tensor_mul(lnS[:, 7:8], lnS[:, 1:2], lnS[:, 5:6])
                        vector.wait_ge(sT, C["sT"])
                        inc(vector.tensor_scalar(q_tok[:], xsl, lnS[:, 5:6], lnS[:, 7:8],
                                                 OP.mult, OP.add), "sV")
                    v_y2 = C["sV"]

                    t_tr_base = C["sT"]
                    s_ev_base = C["sS"]

                    @blk.tensor
                    def _(tensor):
                        tensor.wait_ge(sV, v_y2)
                        for half in range(2):
                            if half == 0:
                                wait_free(tensor, "pt")
                            else:
                                tensor.wait_ge(sS, s_ev_base + 1)
                            last = None
                            for k in range(4):
                                hc = half * 4 + k
                                last = tensor.transpose(pt[:, k * 128:(k + 1) * 128],
                                                        q_tok[:, hc * 128:(hc + 1) * 128],
                                                        identb[:])
                            inc(last, "sT")

                    @blk_scalar
                    def _(scalar):
                        for half in range(2):
                            scalar.wait_ge(sT, t_tr_base + half + 1)
                            hc0 = half * 4
                            inc(scalar.copy(
                                bass.AP(y2T, hc0 * S + tile * 128,
                                        [[NC * S, P], [S, 4], [1, 128]]),
                                bass.AP(pt, 0, [[512, P], [128, 4], [1, 128]])), "sS")
                    free["pt"] = ("sS", C["sS"])

                s_y2T_done = C["sS"]

                with (
                    nc.sbuf_tensor("wslab_v", [P, NC * DM], BF16) as wslab,
                    nc.sbuf_tensor("brow", [1, DM], F32) as brow,
                ):
                    tk_wv = [None]

                    @blk.gpsimd
                    def _(g):
                        g.wait_ge(sV, v_ffn_done)
                        tk_wv[0] = QG.issue(g, [
                            lambda: g.dma_start(
                                bass.AP(wslab, 0, [[NC * DM, P], [DM, NC], [1, DM]]),
                                bass.AP(dWv, 0, [[DM, P], [P * DM, NC], [1, DM]])),
                            lambda: g.dma_start(brow[0:1, 0:DM], dbv.ap())])

                    for h in range(4):
                        if h == 2:
                            t_snap = C["sT"]

                            @blk.gpsimd
                            def _(g):
                                g.wait_ge(sT, t_snap)
                                tk_wv[0] = QG.issue(g, [
                                    lambda: g.dma_start(brow[0:1, 0:DM], dbo.ap())])

                        @blk.tensor
                        def _(tensor):
                            QG.wait(tensor, tk_wv[0])
                            wait_free(tensor, "psmall")
                            inc(tensor.matmul(psmall[:, 0:512], onesr[0:1, :],
                                              brow[0:1, (h % 2) * 512:(h % 2 + 1) * 512],
                                              start=True, stop=True), "sT")
                        t_h = C["sT"]

                        @blk_vector
                        def _(vector):
                            vector.wait_ge(sT, t_h)
                            dst = bvrep if h < 2 else borep
                            inc(vector.tensor_copy(dst[:, (h % 2) * 512:(h % 2 + 1) * 512],
                                                   psmall[:, 0:512]), "sV")
                        free["psmall"] = ("sV", C["sV"])

                    @blk_vector
                    def _(vector):
                        inc(vector.memset(vaug[:], 1.0), "sV")
                    v_ones = C["sV"]

                    t_v_base = C["sT"]

                    @blk.tensor
                    def _(tensor):
                        tensor.wait_ge(sS, s_y2T_done)
                        QG.wait(tensor, tk_wv[0])
                        tensor.wait_ge(sV, v_ones)
                        for tc in range(NT):
                            for half in range(2):
                                psv = pg0 if (2 * tc + half) % 2 == 0 else pg1
                                idx = 2 * tc + half
                                if idx >= 2:
                                    tensor.wait_ge(sV, v_ones + idx - 1)
                                last = None
                                for dc in range(NC):
                                    last = tensor.matmul(
                                        psv[:], y2T[:, dc * S + tc * 128:dc * S + (tc + 1) * 128],
                                        wslab[:, dc * DM + half * 512:dc * DM + (half + 1) * 512],
                                        start=(dc == 0), stop=(dc == NC - 1))
                                inc(last, "sT")

                    @blk_vector
                    def _(vector):
                        for tc in range(NT):
                            for half in range(2):
                                psv = pg0 if (2 * tc + half) % 2 == 0 else pg1
                                idx = 2 * tc + half
                                vector.wait_ge(sT, t_v_base + idx + 1)
                                inc(vector.tensor_add(
                                    bass.AP(vaug, tc * 1040 + half * 520,
                                            [[NT * 1040, P], [65, 8], [1, 64]]),
                                    bass.AP(bvrep, half * 512, [[DM, P], [64, 8], [1, 64]]),
                                    psv[:]), "sV")
                    v_vaug_done = C["sV"]

                with (
                    nc.sbuf_tensor("wqp", [P, NC * 128], BF16) as wqp,
                    nc.sbuf_tensor("wkp", [P, NC * 128], BF16) as wkp,
                    nc.sbuf_tensor("Qp", [P, S], BF16) as Qp,
                    nc.sbuf_tensor("Kp", [P, S], BF16) as Kp,
                    nc.sbuf_tensor("Ebuf", [P, 2048], BF16) as Ebuf,
                    nc.sbuf_tensor("zi", [P, 1024], BF16) as zi,
                    nc.sbuf_tensor("zrA", [P, 512], BF16) as zrA,
                    nc.sbuf_tensor("zrB", [P, 512], BF16) as zrB,
                    nc.sbuf_tensor("nbs", [P, 1024], BF16) as nbs,
                ):
                    nb_dma_hist = []
                    for c in range(NPAIR):
                        tk_qk = [None]

                        @blk.gpsimd
                        def _(g):
                            if c == 0:
                                g.wait_ge(sV, v_vaug_done)
                            else:
                                g.wait_ge(sV, C["sV"])
                            tk_qk[0] = QG.issue(g, [
                                lambda: g.dma_start(
                                    bass.AP(wqp, 0, [[NC * 128, P], [128, NC], [1, 128]]),
                                    bass.AP(dWq, 128 * c, [[DM, P], [P * DM, NC], [1, 128]])),
                                lambda: g.dma_start(
                                    bass.AP(wkp, 0, [[NC * 128, P], [128, NC], [1, 128]]),
                                    bass.AP(dWk, 128 * c, [[DM, P], [P * DM, NC], [1, 128]]))])

                        t_qk_base = C["sT"]
                        s_qk_pre = C["sS"]

                        @blk.tensor
                        def _(tensor):
                            QG.wait(tensor, tk_qk[0])
                            tensor.wait_ge(sV, C["sV"] if c > 0 else v_vaug_done)
                            for s4 in range(4):
                                if s4 >= 1:
                                    tensor.wait_ge(sS, s_qk_pre + 2 * s4)
                                last = None
                                for dc in range(NC):
                                    last = tensor.matmul(
                                        pg0[:], wqp[:, dc * 128:(dc + 1) * 128],
                                        y2T[:, dc * S + s4 * 512:dc * S + (s4 + 1) * 512],
                                        start=(dc == 0), stop=(dc == NC - 1))
                                inc(last, "sT")
                                for dc in range(NC):
                                    last = tensor.matmul(
                                        pg1[:], wkp[:, dc * 128:(dc + 1) * 128],
                                        y2T[:, dc * S + s4 * 512:dc * S + (s4 + 1) * 512],
                                        start=(dc == 0), stop=(dc == NC - 1))
                                inc(last, "sT")

                        @blk_scalar
                        def _(scalar):
                            for s4 in range(4):
                                scalar.wait_ge(sT, t_qk_base + 2 * s4 + 1)
                                scalar.activation(Qp[:, s4 * 512:(s4 + 1) * 512], pg0[:],
                                                  AF.Identity, bias=bcols[:, c:c + 1],
                                                  scale=f32(1.0))
                                scalar.wait_ge(sT, t_qk_base + 2 * s4 + 2)
                                inc(scalar.activation(Kp[:, s4 * 512:(s4 + 1) * 512], pg1[:],
                                                      AF.Identity, bias=bcols[:, NC + c:NC + c + 1],
                                                      scale=f32(1.0)), "sS", 2)
                        s_qk = C["sS"]

                        for s4 in range(4):
                            kcs = list(range(4 * s4 + 4))
                            nkc = len(kcs)
                            t_core_base = C["sT"]
                            s_core_base = C["sS"]
                            v_core_base = C["sV"]

                            @blk.tensor
                            def _(tensor):
                                tensor.wait_ge(sS, s_qk)
                                for i4, kc in enumerate(kcs):
                                    o = max(0, 128 * kc - 512 * s4)
                                    if i4 >= 1:
                                        tensor.wait_ge(sS, s_core_base + 2 * i4)
                                    tensor.matmul(pg0[:, o:512],
                                                  Kp[0:64, kc * 128:(kc + 1) * 128],
                                                  Qp[0:64, s4 * 512 + o:(s4 + 1) * 512],
                                                  start=True, stop=True)
                                    inc(tensor.matmul(pg1[:, o:512],
                                                      Kp[64:128, kc * 128:(kc + 1) * 128],
                                                      Qp[64:128, s4 * 512 + o:(s4 + 1) * 512],
                                                      start=True, stop=True), "sT")
                                    tensor.wait_ge(sV, v_core_base + i4 + 1)
                                    eb = (i4 % 2) * 1024
                                    tensor.matmul(pu0[0:65, o:512],
                                                  bass.AP(vaug, kc * 1040 + 65 * (2 * c),
                                                          [[NT * 1040, P], [1, 65]]),
                                                  Ebuf[:, eb + o:eb + 512],
                                                  start=(i4 == 0), stop=(i4 == nkc - 1))
                                    inc(tensor.matmul(pu1[0:65, o:512],
                                                      bass.AP(vaug, kc * 1040 + 65 * (2 * c + 1),
                                                              [[NT * 1040, P], [1, 65]]),
                                                      Ebuf[:, eb + 512 + o:eb + 1024],
                                                      start=(i4 == 0), stop=(i4 == nkc - 1)),
                                        "sT")
                            t_pv_done = C["sT"]

                            @blk_scalar
                            def _(scalar):
                                for i4, kc in enumerate(kcs):
                                    o = max(0, 128 * kc - 512 * s4)
                                    eb = (i4 % 2) * 1024
                                    scalar.wait_ge(sT, t_core_base + 2 * i4 + 1)
                                    if i4 >= 2:
                                        scalar.wait_ge(sT, t_core_base + 2 * (i4 - 1))
                                    scalar.activation(Ebuf[:, eb + o:eb + 512], pg0[:, o:512],
                                                      AF.Exp, scale=f32(0.125), bias=f32(0.0))
                                    inc(scalar.activation(Ebuf[:, eb + 512 + o:eb + 1024],
                                                          pg1[:, o:512], AF.Exp,
                                                          scale=f32(0.125), bias=f32(0.0)),
                                        "sS", 2)

                            @blk_vector
                            def _(vector):
                                for i4, kc in enumerate(kcs):
                                    o = max(0, 128 * kc - 512 * s4)
                                    eb = (i4 % 2) * 1024
                                    vector.wait_ge(sS, s_core_base + 2 * (i4 + 1))
                                    if kc >= 4 * s4:
                                        vector.tensor_mul(Ebuf[:, eb + o:eb + o + 128],
                                                          Ebuf[:, eb + o:eb + o + 128], tri[:])
                                        vector.tensor_mul(
                                            Ebuf[:, eb + 512 + o:eb + 512 + o + 128],
                                            Ebuf[:, eb + 512 + o:eb + 512 + o + 128], tri[:])
                                    inc(vector.engine_nop(), "sV")

                            @blk_vector
                            def _(vector):
                                vector.wait_ge(sT, t_pv_done)
                                with nc.allow_low_precision(reason="1/Z as bf16 matmul operand"):
                                    vector.reciprocal(zi[64:65, 0:512], pu0[64:65, :])
                                    inc(vector.reciprocal(zi[64:65, 512:1024], pu1[64:65, :]),
                                        "sV")
                            v_zi = C["sV"]

                            @blk.tensor
                            def _(tensor):
                                tensor.wait_ge(sV, v_zi)
                                tensor.matmul(pd0[0:64, :], onesbb[64:65, :],
                                              zi[64:65, 0:512], start=True, stop=True)
                                inc(tensor.matmul(pd1[0:64, :], onesbb[64:65, :],
                                                  zi[64:65, 512:1024], start=True, stop=True),
                                    "sT")
                            t_z = C["sT"]

                            nbh = (4 * c + s4) % 2

                            @blk_scalar
                            def _(scalar):
                                scalar.wait_ge(sT, t_z)
                                scalar.copy(zrA[0:64, :], pd0[0:64, :])
                                inc(scalar.copy(zrB[0:64, :], pd1[0:64, :]), "sS")
                            s_zr = C["sS"]

                            @blk_vector
                            def _(vector):
                                vector.wait_ge(sS, s_zr)
                                vector.tensor_mul(
                                    anorm[0:64, c * S + s4 * 512:c * S + (s4 + 1) * 512],
                                    pu0[0:64, :], zrA[0:64, :])
                                if len(nb_dma_hist) >= 2:
                                    QG.wait(vector, nb_dma_hist[-2])
                                inc(vector.tensor_mul(nbs[0:64, nbh * 512:(nbh + 1) * 512],
                                                      pu1[0:64, :], zrB[0:64, :]), "sV")
                            v_nb = C["sV"]

                            @blk.gpsimd
                            def _(g):
                                g.wait_ge(sV, v_nb)
                                nb_dma_hist.append(QG.issue(g, [lambda: g.dma_start(
                                    anorm[64:128, c * S + s4 * 512:c * S + (s4 + 1) * 512],
                                    nbs[0:64, nbh * 512:(nbh + 1) * 512])]))

            v_pair_done = C["sV"]
            with nc.sbuf_tensor("wslab_o", [P, NC * DM], BF16) as wslab_o:
                tk_wo = [None]

                @blk.gpsimd
                def _(g):
                    g.wait_ge(sV, v_pair_done)
                    tk_wo[0] = QG.issue(g, [lambda: g.dma_start(
                        bass.AP(wslab_o, 0, [[NC * DM, P], [DM, NC], [1, DM]]),
                        bass.AP(dWo, 0, [[DM, P], [P * DM, NC], [1, DM]]))])

                t_op_base = C["sT"]

                @blk.tensor
                def _(tensor):
                    QG.wait_all(tensor)
                    tensor.wait_ge(sV, v_pair_done)
                    for tc in range(NT):
                        for half in range(2):
                            pdx = pd0 if (2 * tc + half) % 2 == 0 else pd1
                            idx = 2 * tc + half
                            if idx >= 2:
                                tensor.wait_ge(sV, v_pair_done + idx - 1)
                            last = None
                            for cc in range(NPAIR):
                                last = tensor.matmul(
                                    pdx[:], anorm[:, cc * S + tc * 128:cc * S + (tc + 1) * 128],
                                    wslab_o[:, cc * DM + half * 512:cc * DM + (half + 1) * 512],
                                    start=(cc == 0), stop=(cc == NPAIR - 1))
                            inc(last, "sT")

                @blk_vector
                def _(vector):
                    for tc in range(NT):
                        for half in range(2):
                            pdx = pd0 if (2 * tc + half) % 2 == 0 else pd1
                            idx = 2 * tc + half
                            vector.wait_ge(sT, t_op_base + idx + 1)
                            xs2 = x_sb[:, tc * DM + half * 512:tc * DM + (half + 1) * 512]
                            vector.tensor_add(m1sb[:], pdx[:],
                                              borep[:, half * 512:(half + 1) * 512])
                            inc(vector.tensor_add(xs2, xs2, m1sb[:]), "sV")

    emit_ffn(0)
    emit_attn()
    emit_ffn(1)

    v_all = C["sV"]

    @blk.sync
    def _(sync):
        sync.wait_ge(sV, v_all)
        tko = QA.issue(sync, [lambda: sync.dma_start(
            bass.AP(dout, 0, [[DM, P], [P * DM, NT], [1, DM]]),
            bass.AP(x_sb, 0, [[NT * DM, P], [DM, NT], [1, DM]]))])
        QA.wait(sync, tko)
        QG.wait_all(sync)

    es.close()
    return nc


_NC_CACHE = None


def kernel(L0_attn_x, L0_attn_Wq, L0_attn_bq, L0_attn_Wk, L0_attn_bk,
           L0_attn_Wv, L0_attn_bv, L0_attn_Wo, L0_attn_bo,
           L0_ffn_w_gate, L0_ffn_w_up, L0_ffn_w_down):
    global _NC_CACHE
    if _NC_CACHE is None:
        _NC_CACHE = build_core()
    nc = _NC_CACHE

    x = np.ascontiguousarray(np.asarray(L0_attn_x, dtype=np.float32))
    shared = dict(
        Wq=np.ascontiguousarray(np.asarray(L0_attn_Wq, dtype=np.float32)),
        Wk=np.ascontiguousarray(np.asarray(L0_attn_Wk, dtype=np.float32)),
        Wv=np.ascontiguousarray(np.asarray(L0_attn_Wv, dtype=np.float32)),
        Wo=np.ascontiguousarray(np.asarray(L0_attn_Wo, dtype=np.float32)),
        bq=np.ascontiguousarray(np.asarray(L0_attn_bq, dtype=np.float32).reshape(1, DM)),
        bk=np.ascontiguousarray(np.asarray(L0_attn_bk, dtype=np.float32).reshape(1, DM)),
        bv=np.ascontiguousarray(np.asarray(L0_attn_bv, dtype=np.float32).reshape(1, DM)),
        bo=np.ascontiguousarray(np.asarray(L0_attn_bo, dtype=np.float32).reshape(1, DM)),
        wg=np.ascontiguousarray(np.asarray(L0_ffn_w_gate, dtype=np.float32)),
        wu=np.ascontiguousarray(np.asarray(L0_ffn_w_up, dtype=np.float32)),
        wd=np.ascontiguousarray(np.asarray(L0_ffn_w_down, dtype=np.float32)),
        idb=np.eye(P, dtype=ml_dtypes.bfloat16),
        idf=np.eye(P, dtype=np.float32),
        tri=np.triu(np.ones((P, P), dtype=ml_dtypes.bfloat16)),
        onesf=np.ones((1, P), dtype=np.float32),
        onesb=np.ones((1, P), dtype=ml_dtypes.bfloat16),
    )
    in_maps = [dict(x=np.ascontiguousarray(x[i]), **shared) for i in range(8)]
    res = run_bass_kernel_spmd(nc, in_maps, core_ids=list(range(8)))
    out = np.stack([np.asarray(r["out"], dtype=np.float32) for r in res.results], axis=0)
    return out


# version 04
# speedup vs baseline: 1.0087x; 1.0087x over previous
import numpy as np
import ml_dtypes
from contextlib import ExitStack

import concourse.bass as bass
import concourse.mybir as mybir
from concourse.bass_utils import run_bass_kernel_spmd

F32 = mybir.dt.float32
BF16 = mybir.dt.bfloat16
I32 = mybir.dt.int32
AF = mybir.ActivationFunctionType
OP = mybir.AluOpType
AX = mybir.AxisListType
f32 = np.float32

P = 128
S = 2048
NT = 16
DM = 1024
NC = 8
II = 4096
NI = 32
TB = 512
NB = 4
NPAIR = 8
EPS = 1e-5


_VOPS = ("tensor_scalar", "tensor_tensor", "tensor_add", "tensor_mul",
         "tensor_copy", "reduce_sum", "reduce_max", "reciprocal", "memset",
         "scalar_tensor_tensor", "engine_nop")


class _PubInstr:

    def __init__(self, eng, instr):
        self._eng = eng
        self._i = instr

    def then_inc(self, sem, n):
        d = self._eng.drain()
        d.then_inc(sem, n)
        return d


class _DV:

    def __init__(self, eng):
        self._e = eng

    def __getattr__(self, name):
        attr = getattr(self._e, name)
        if name in _VOPS:
            e = self._e

            def wrapped(*a, **k):
                e.drain()
                return attr(*a, **k)
            return wrapped
        return attr


_SOPS = ("activation", "copy")


class _DS:

    def __init__(self, eng):
        self._e = eng

    def __getattr__(self, name):
        attr = getattr(self._e, name)
        if name in _SOPS:
            e = self._e

            def wrapped(*a, **k):
                return _PubInstr(e, attr(*a, **k))
            return wrapped
        return attr


def build_core():
    nc = bass.Bass("TRN2", target_bir_lowering=False, detect_race_conditions=False)
    es = ExitStack()
    C = {"sdA": 0, "sdG": 0, "sT": 0, "sV": 0, "sS": 0}

    dx = nc.dram_tensor("x", [S, DM], F32, kind="ExternalInput")
    dWq = nc.dram_tensor("Wq", [DM, DM], F32, kind="ExternalInput")
    dWk = nc.dram_tensor("Wk", [DM, DM], F32, kind="ExternalInput")
    dWv = nc.dram_tensor("Wv", [DM, DM], F32, kind="ExternalInput")
    dWo = nc.dram_tensor("Wo", [DM, DM], F32, kind="ExternalInput")
    dbq = nc.dram_tensor("bq", [1, DM], F32, kind="ExternalInput")
    dbk = nc.dram_tensor("bk", [1, DM], F32, kind="ExternalInput")
    dbv = nc.dram_tensor("bv", [1, DM], F32, kind="ExternalInput")
    dbo = nc.dram_tensor("bo", [1, DM], F32, kind="ExternalInput")
    dwg = nc.dram_tensor("wg", [II, DM], F32, kind="ExternalInput")
    dwu = nc.dram_tensor("wu", [II, DM], F32, kind="ExternalInput")
    dwd = nc.dram_tensor("wd", [DM, II], F32, kind="ExternalInput")
    didb = nc.dram_tensor("idb", [P, P], BF16, kind="ExternalInput")
    didf = nc.dram_tensor("idf", [P, P], F32, kind="ExternalInput")
    dtri = nc.dram_tensor("tri", [P, P], BF16, kind="ExternalInput")
    dones = nc.dram_tensor("onesf", [1, P], F32, kind="ExternalInput")
    donesb = nc.dram_tensor("onesb", [1, P], BF16, kind="ExternalInput")
    dout = nc.dram_tensor("out", [S, DM], F32, kind="ExternalOutput")
    dswg = nc.dram_tensor("swg", [II * NC, P], BF16, kind="Internal")
    dswu = nc.dram_tensor("swu", [II * NC, P], BF16, kind="Internal")
    dswd = nc.dram_tensor("swd", [II * NC, P], BF16, kind="Internal")

    def sb(name, shape, dt):
        return es.enter_context(nc.sbuf_tensor(name, shape, dt))

    def ps_(name, shape, dt):
        return es.enter_context(nc.psum_tensor(name, shape, dt))

    identb = sb("identb", [P, P], BF16)
    identf = sb("identf", [P, P], F32)
    tri = sb("tri_sb", [P, P], BF16)
    onesr = sb("onesr", [P, P], F32)
    onesbb = sb("onesbb", [P, 64], BF16)
    bcols = sb("bcols", [P, 4 * NC], F32)
    mw = sb("mw", [1, 8], F32)
    k1bc = sb("k1bc", [P, 1], F32)
    k2bc = sb("k2bc", [P, 1], F32)
    onescol = sb("onescol", [P, 1], F32)

    pg0 = ps_("pg0", [P, 512], F32)
    pg1 = ps_("pg1", [P, 512], F32)
    pu0 = ps_("pu0", [P, 512], F32)
    pu1 = ps_("pu1", [P, 512], F32)
    pd0 = ps_("pd0", [P, 512], F32)
    pd1 = ps_("pd1", [P, 512], F32)
    psmall = ps_("psmall", [P, 512], F32)
    pt = ps_("pt", [P, 512], BF16)

    sdA = es.enter_context(nc.semaphore("sdA"))
    sdG = es.enter_context(nc.semaphore("sdG"))
    sT = es.enter_context(nc.semaphore("sT"))
    sV = es.enter_context(nc.semaphore("sV"))
    sS = es.enter_context(nc.semaphore("sS"))
    SEM = {"sdA": sdA, "sdG": sdG, "sT": sT, "sV": sV, "sS": sS}

    def inc(instr, s, n=1):
        C[s] += n
        instr.then_inc(SEM[s], n)
        return C[s]

    class DmaQ:

        def __init__(self, name, n=4):
            self.sems = [es.enter_context(nc.semaphore(f"{name}{i}")) for i in range(n)]
            self.tot = [0] * n
            self.i = 0

        def issue(self, eng, fns):
            j = self.i % len(self.sems)
            self.i += 1
            eng.wait_ge(self.sems[j], self.tot[j])
            for fn in fns:
                instr = fn()
                self.tot[j] += 16
                instr.then_inc(self.sems[j], 16)
            return (j, self.tot[j])

        def wait(self, eng, ticket):
            eng.wait_ge(self.sems[ticket[0]], ticket[1])

        def wait_all(self, eng):
            for j, s in enumerate(self.sems):
                eng.wait_ge(s, self.tot[j])

    QA = DmaQ("qa")
    QG = DmaQ("qg")

    free = {"pt": None, "psmall": None}

    def wait_free(eng, key):
        if free[key] is not None:
            eng.wait_ge(SEM[free[key][0]], free[key][1])

    blk = es.enter_context(nc.Block())

    _real_vec_dec = blk.vector
    _real_sca_dec = blk.scalar

    def vecblk(fn):
        return _real_vec_dec(lambda vector: fn(_DV(vector)))
    blk_vector = vecblk

    def scablk(fn):
        return _real_sca_dec(fn)
    blk_scalar = scablk

    tk_consts = [None]

    @blk.sync
    def _(sync):
        fns = [lambda d=dst, s=srcd: sync.dma_start(d[:], s.ap())
               for dst, srcd in ((identb, didb), (identf, didf), (tri, dtri))]
        fns.append(lambda: sync.dma_start(onesr[0:1, :], dones.ap()))
        fns.append(lambda: sync.dma_start(onesbb[64:65, :],
                                          bass.AP(donesb, 0, [[1, 1], [1, 64]])))
        for i4, db in enumerate((dbq, dbk, dbv, dbo)):
            fns.append(lambda i4=i4, db=db: sync.dma_start(
                bass.AP(bcols, 8 * i4, [[4 * NC, P], [1, NC]]),
                bass.AP(db, 0, [[1, P], [128, NC]]),
                allow_slow_non_contiguous=True))
        tk_consts[0] = QA.issue(sync, fns)

    @blk_vector
    def _(vector):
        QA.wait(vector, tk_consts[0])
        inc(vector.memset(onescol[:], 1.0), "sV")

    @blk.tensor
    def _(tensor):
        QA.wait(tensor, tk_consts[0])

    @blk_scalar
    def _(scalar):
        QA.wait(scalar, tk_consts[0])

    def quant_weight(widx, dsrc, dscr, is_wd):
        with (
            nc.sbuf_tensor(f"W32_{widx}", [P, 32768], F32) as W32,
            nc.sbuf_tensor(f"sgA_{widx}", [P, 4096], F32) as sgA,
            nc.sbuf_tensor(f"sgB_{widx}", [P, 4096], F32) as sgB,
            nc.sbuf_tensor(f"tq_{widx}", [P, 8192], BF16) as tq,
            nc.sbuf_tensor(f"st2_{widx}", [P, 8192], BF16) as st2,
            nc.sbuf_tensor(f"prt_{widx}", [P, 10], F32) as prt,
            nc.sbuf_tensor(f"acc8_{widx}", [P, 8], F32) as acc8,
            nc.sbuf_tensor(f"sbc_{widx}", [P, 1], F32) as sbc,
        ):
            vw = C["sV"]
            sw = C["sS"]
            tw = C["sT"]
            w32_tk = []

            @blk.sync
            def _(sync):
                sync.wait_ge(sV, vw)
                sync.wait_ge(sS, sw)
                sync.wait_ge(sT, tw)
                QG.wait_all(sync)
                for j in range(4):
                    if is_wd:
                        fn = lambda j=j: sync.dma_start(
                            bass.AP(W32, j * 8192, [[32768, P], [4096, 2], [1, 4096]]),
                            bass.AP(dsrc, j * 2 * P * II, [[II, P], [P * II, 2], [1, II]]))
                    else:
                        fn = lambda j=j: sync.dma_start(
                            bass.AP(W32, j * 8192, [[32768, P], [1024, 8], [1, 1024]]),
                            bass.AP(dsrc, j * 8 * P * DM, [[DM, P], [P * DM, 8], [1, DM]]))
                    w32_tk.append(QA.issue(sync, [fn]))

            @blk_vector
            def _(vector):
                for j in range(4):
                    QA.wait(vector, w32_tk[j])
                    vector.reduce_sum(prt[:, 2 * j:2 * j + 1],
                                      W32[:, j * 8192:j * 8192 + 4096],
                                      AX.X, apply_absolute_value=True)
                    r = vector.reduce_sum(prt[:, 2 * j + 1:2 * j + 2],
                                          W32[:, j * 8192 + 4096:(j + 1) * 8192],
                                          AX.X, apply_absolute_value=True)
                    if j == 3:
                        inc(r, "sV")
            v_sum = C["sV"]

            @blk_scalar
            def _(scalar):
                scalar.wait_ge(sV, v_sum)
                inc(scalar.activation(acc8[:], prt[:, 0:8], AF.Identity,
                                      accum_out=prt[:, 8:9]), "sS")
            s_tot8 = C["sS"]

            @blk.tensor
            def _(tensor):
                tensor.wait_ge(sS, s_tot8)
                wait_free(tensor, "psmall")
                inc(tensor.matmul(psmall[0:1, 0:1], prt[:, 8:9], onescol[:, 0:1],
                                  start=True, stop=True), "sT")
            t_tot = C["sT"]

            @blk_vector
            def _(vector):
                vector.wait_ge(sT, t_tot)
                inc(vector.tensor_scalar(mw[0:1, widx:widx + 1], psmall[0:1, 0:1],
                                         1.0 / 4194304.0, EPS, OP.mult, OP.max), "sV")
            v_mw = C["sV"]

            @blk_scalar
            def _(scalar):
                scalar.wait_ge(sV, v_mw)
                inc(scalar.copy(mw[0:1, 6:7], mw[0:1, widx:widx + 1]), "sS")
            s_mw = C["sS"]

            @blk_vector
            def _(vector):
                vector.wait_ge(sS, s_mw)
                vector.reciprocal(mw[0:1, 7:8], mw[0:1, 6:7])
                vector.tensor_mul(mw[0:1, 5:6], mw[0:1, 6:7], mw[0:1, 7:8])
                vector.tensor_scalar(mw[0:1, 5:6], mw[0:1, 5:6], -1.0, 2.0,
                                     OP.mult, OP.add)
                inc(vector.tensor_mul(mw[0:1, 7:8], mw[0:1, 7:8], mw[0:1, 5:6]), "sV")
            v_m = C["sV"]

            @blk.tensor
            def _(tensor):
                tensor.wait_ge(sV, v_m)
                inc(tensor.matmul(psmall[:, 0:1], onesr[0:1, :], mw[0:1, 7:8],
                                  start=True, stop=True), "sT")
            t_bc = C["sT"]

            @blk_vector
            def _(vector):
                vector.wait_ge(sT, t_bc)
                inc(vector.tensor_copy(sbc[:], psmall[:, 0:1]), "sV")
            v_sbc = C["sV"]
            free["psmall"] = ("sV", v_sbc)

            tq_tdone = [None] * 8
            st2_gdone = [None] * 8
            for j in range(8):
                jb = j % 2

                @blk_scalar
                def _(scalar):
                    scalar.wait_ge(sV, v_sbc)
                    if j >= 1:
                        scalar.wait_ge(sV, C["sV"])
                    scalar.activation(sgA[:], W32[:, j * 4096:(j + 1) * 4096], AF.Sign,
                                      bias=f32(-0.5), scale=sbc[:, 0:1])
                    inc(scalar.activation(sgB[:], W32[:, j * 4096:(j + 1) * 4096], AF.Sign,
                                          bias=f32(0.5), scale=sbc[:, 0:1]), "sS")
                s_sg = C["sS"]

                @blk_vector
                def _(vector):
                    vector.wait_ge(sS, s_sg)
                    if j >= 2 and tq_tdone[j - 2] is not None:
                        vector.wait_ge(sT, tq_tdone[j - 2])
                    inc(vector.tensor_add(tq[:, jb * 4096:(jb + 1) * 4096], sgA[:], sgB[:]), "sV")
                v_tq = C["sV"]

                t_tr_base = C["sT"]

                @blk.tensor
                def _(tensor):
                    tensor.wait_ge(sV, v_tq)
                    for bat in range(8):
                        if bat == 0:
                            wait_free(tensor, "pt")
                        else:
                            tensor.wait_ge(sS, s_sg + bat)
                        last = None
                        for k in range(4):
                            bi = bat * 4 + k
                            last = tensor.transpose(
                                pt[:, k * 128:(k + 1) * 128],
                                tq[:, jb * 4096 + bi * 128:jb * 4096 + (bi + 1) * 128],
                                identb[:])
                        inc(last, "sT")
                tq_tdone[j] = C["sT"]

                @blk_scalar
                def _(scalar):
                    for bat in range(8):
                        scalar.wait_ge(sT, t_tr_base + bat + 1)
                        if j >= 2 and bat == 0:
                            QG.wait(scalar, st2_gdone[j - 2])
                        inc(scalar.copy(st2[:, jb * 4096 + bat * 512:jb * 4096 + (bat + 1) * 512],
                                        pt[:, 0:512]), "sS")
                s_evb = C["sS"]
                free["pt"] = ("sS", s_evb)

                @blk.gpsimd
                def _(g):
                    g.wait_ge(sS, s_evb)
                    st2_gdone[j] = QG.issue(g, [lambda: g.dma_start(
                        bass.AP(dscr, (j * 32) * 128 * P, [[P, P], [128 * P, 32], [1, P]]),
                        bass.AP(st2, jb * 4096, [[8192, P], [128, 32], [1, P]]))])

    quant_weight(0, dwg, dswg, False)
    quant_weight(1, dwu, dswu, False)
    quant_weight(2, dwd, dswd, True)

    x_sb = sb("x_sb", [P, NT * DM], F32)
    lnA = sb("lnA", [P, DM], F32)
    lnY = sb("lnY", [P, DM], F32)
    lnS = sb("lnS", [P, 16], F32)
    acol = sb("acol", [P, 4], F32)
    q_tok = sb("q_tok", [P, DM], BF16)
    m1sb = sb("m1sb", [P, 512], F32)
    sbrow = sb("sbrow", [1, TB], F32)
    ocols = sb("ocols", [P, 4], F32)

    @blk_vector
    def _(vector):
        vector.tensor_mul(mw[0:1, 3:4], mw[0:1, 0:1], mw[0:1, 0:1])
        vector.tensor_mul(mw[0:1, 3:4], mw[0:1, 3:4], mw[0:1, 1:2])
        vector.tensor_scalar(mw[0:1, 3:4], mw[0:1, 3:4], 1.0 / (127.0 ** 3 * 8.0), None, OP.mult)
        inc(vector.tensor_scalar(mw[0:1, 4:5], mw[0:1, 2:3], 0.5 / (127.0 * 2.0), None, OP.mult), "sV")
    v_k = C["sV"]

    @blk.tensor
    def _(tensor):
        tensor.wait_ge(sV, v_k)
        wait_free(tensor, "psmall")
        tensor.matmul(psmall[:, 0:1], onesr[0:1, :], mw[0:1, 3:4], start=True, stop=True)
        inc(tensor.matmul(psmall[:, 1:2], onesr[0:1, :], mw[0:1, 4:5], start=True, stop=True), "sT")
    t_k = C["sT"]

    @blk_vector
    def _(vector):
        vector.wait_ge(sT, t_k)
        vector.tensor_copy(k1bc[:], psmall[:, 0:1])
        inc(vector.tensor_copy(k2bc[:], psmall[:, 1:2]), "sV")
    free["psmall"] = ("sV", C["sV"])

    tk_x = [None]

    vq_done, sq_done, tq_done = C["sV"], C["sS"], C["sT"]

    @blk.sync
    def _(sync):
        sync.wait_ge(sV, vq_done)
        sync.wait_ge(sS, sq_done)
        sync.wait_ge(sT, tq_done)
        QG.wait_all(sync)
        tk_x[0] = QA.issue(sync, [lambda: sync.dma_start(
            bass.AP(x_sb, 0, [[NT * DM, P], [DM, NT], [1, DM]]),
            bass.AP(dx, 0, [[DM, P], [P * DM, NT], [1, DM]]))])

    def emit_ffn(call):
        with (
            nc.sbuf_tensor(f"wgr{call}", [P, 8 * 1024], BF16) as wgr,
            nc.sbuf_tensor(f"wur{call}", [P, 8 * 1024], BF16) as wur,
            nc.sbuf_tensor(f"wdr{call}", [P, 8 * 1024], BF16) as wdr,
            nc.sbuf_tensor(f"qT{call}", [P, NC * TB], BF16) as qT,
            nc.sbuf_tensor(f"harena{call}", [P, NI * TB], BF16) as harena,
            nc.sbuf_tensor(f"maxrun{call}", [P, TB], BF16) as maxrun,
            nc.sbuf_tensor(f"minrun{call}", [P, TB], BF16) as minrun,
            nc.sbuf_tensor(f"lnQi{call}", [P, DM], I32) as lnQi,
            nc.sbuf_tensor(f"lnCr{call}", [P, DM], F32) as lnCr,
            nc.sbuf_tensor(f"Rsb{call}", [P, 512], F32) as Rsb,
            nc.sbuf_tensor(f"R2sb{call}", [P, 512], F32) as R2sb,
        ):
            @blk.sync
            def _(sync):
                sync.wait_ge(sV, C["sV"])
                sync.wait_ge(sS, C["sS"])
                sync.wait_ge(sT, C["sT"])
                if call == 0:
                    QG.wait_all(sync)

            t_gu_hist = []
            t_dn_hist = []

            for b in range(NB):
                for j in range(4):
                    tile = 4 * b + j
                    xsl = x_sb[:, tile * DM:(tile + 1) * DM]

                    @blk_vector
                    def _(vector):
                        if call == 0 and b == 0 and j == 0:
                            QA.wait(vector, tk_x[0])
                        vector.reduce_sum(lnS[:, 0:1], xsl, AX.X)
                        inc(vector.tensor_scalar(lnS[:, 1:2], lnS[:, 0:1], -1.0 / DM,
                                                 None, OP.mult), "sV")
                    v_m = C["sV"]

                    @blk_scalar
                    def _(scalar):
                        scalar.wait_ge(sV, v_m)
                        inc(scalar.activation(lnA[:], xsl, AF.Square, bias=lnS[:, 1:2],
                                              accum_out=lnS[:, 2:3]), "sS")
                    s_sq = C["sS"]

                    @blk_vector
                    def _(vector):
                        vector.wait_ge(sS, s_sq)
                        inc(vector.tensor_scalar(lnS[:, 3:4], lnS[:, 2:3], 1.0 / DM, EPS,
                                                 OP.mult, OP.add), "sV")
                    v_t0 = C["sV"]

                    @blk_scalar
                    def _(scalar):
                        scalar.wait_ge(sV, v_t0)
                        inc(scalar.activation(lnS[:, 4:5], lnS[:, 3:4], AF.Sqrt), "sS")
                    s_rt = C["sS"]

                    @blk_vector
                    def _(vector):
                        vector.wait_ge(sS, s_rt)
                        vector.reciprocal(lnS[:, 5:6], lnS[:, 4:5])
                        for _it in range(2):
                            vector.tensor_mul(lnS[:, 6:7], lnS[:, 5:6], lnS[:, 5:6])
                            vector.tensor_mul(lnS[:, 6:7], lnS[:, 6:7], lnS[:, 3:4])
                            vector.tensor_scalar(lnS[:, 6:7], lnS[:, 6:7], -0.5, 1.5,
                                                 OP.mult, OP.add)
                            vector.tensor_mul(lnS[:, 5:6], lnS[:, 5:6], lnS[:, 6:7])
                        vector.tensor_mul(lnS[:, 7:8], lnS[:, 1:2], lnS[:, 5:6])
                        vector.tensor_scalar(lnY[:], xsl, lnS[:, 5:6], lnS[:, 7:8],
                                             OP.mult, OP.add)
                        vector.reduce_max(lnS[:, 8:9], lnY[:], AX.X, apply_absolute_value=True)
                        vector.tensor_scalar(lnS[:, 9:10], lnS[:, 8:9], EPS, None, OP.max)
                        vector.reciprocal(lnS[:, 10:11], lnS[:, 9:10])
                        vector.tensor_mul(lnS[:, 12:13], lnS[:, 9:10], lnS[:, 10:11])
                        vector.tensor_scalar(lnS[:, 12:13], lnS[:, 12:13], -1.0, 2.0,
                                             OP.mult, OP.add)
                        vector.tensor_mul(lnS[:, 10:11], lnS[:, 10:11], lnS[:, 12:13])
                        vector.tensor_scalar(lnS[:, 11:12], lnS[:, 10:11], 127.0, None, OP.mult)
                        vector.tensor_copy(acol[:, j:j + 1], lnS[:, 9:10])
                        vector.tensor_scalar(lnA[:], lnY[:], lnS[:, 11:12], 256.0, OP.mult, OP.add)
                        vector.tensor_copy(lnQi[:], lnA[:])
                        vector.scalar_tensor_tensor(lnCr[:], lnQi[:], -255.5, lnA[:],
                                                    OP.add, OP.is_le)
                        vector.wait_ge(sT, C["sT"])
                        inc(vector.scalar_tensor_tensor(q_tok[:], lnQi[:], -256.0, lnCr[:],
                                                        OP.add, OP.add), "sV")
                    v_q = C["sV"]

                    t_tr_base = C["sT"]
                    s_ev_base = C["sS"]

                    @blk.tensor
                    def _(tensor):
                        tensor.wait_ge(sV, v_q)
                        for half in range(2):
                            if half == 0:
                                wait_free(tensor, "pt")
                            else:
                                tensor.wait_ge(sS, s_ev_base + 1)
                            last = None
                            for k in range(4):
                                hc = half * 4 + k
                                last = tensor.transpose(pt[:, k * 128:(k + 1) * 128],
                                                        q_tok[:, hc * 128:(hc + 1) * 128],
                                                        identb[:])
                            inc(last, "sT")

                    @blk_scalar
                    def _(scalar):
                        for half in range(2):
                            scalar.wait_ge(sT, t_tr_base + half + 1)
                            hc0 = half * 4
                            inc(scalar.copy(
                                bass.AP(qT, hc0 * TB + j * 128, [[NC * TB, P], [TB, 4], [1, 128]]),
                                bass.AP(pt, 0, [[512, P], [128, 4], [1, 128]])), "sS")
                    free["pt"] = ("sS", C["sS"])

                v_lnq_done = C["sV"]
                s_qT_done = C["sS"]

                gw_tk = []
                t_gu_pred = C["sT"]

                @blk.sync
                def _(sync):
                    for g in range(16):
                        ic0 = 2 * g
                        if ic0 - 8 >= 0:
                            sync.wait_ge(sT, t_gu_pred + 2 * (ic0 - 8 + 2))
                        elif t_gu_hist:
                            sync.wait_ge(sT, t_gu_hist[-1] + 2 * (24 + ic0 + 2))
                        slot = ic0 % 8
                        fns = []
                        for wsb, dscr in ((wgr, dswg), (wur, dswu)):
                            fns.append(lambda wsb=wsb, dscr=dscr, slot=slot, ic0=ic0:
                                       sync.dma_start(
                                           bass.AP(wsb, slot * 1024,
                                                   [[8 * 1024, P], [P, 16], [1, P]]),
                                           bass.AP(dscr, ic0 * NC * P * P,
                                                   [[P, P], [P * P, 16], [1, P]])))
                        gw_tk.append(QA.issue(sync, fns))

                t_gu_base = C["sT"]
                assert t_gu_base == t_gu_pred
                t_gu_hist.append(t_gu_base)

                @blk.tensor
                def _(tensor):
                    tensor.wait_ge(sS, s_qT_done)
                    tensor.wait_ge(sV, v_lnq_done)
                    for ic in range(NI):
                        pg = pg0 if ic % 2 == 0 else pg1
                        pu = pu0 if ic % 2 == 0 else pu1
                        QA.wait(tensor, gw_tk[ic // 2])
                        if ic >= 2:
                            tensor.wait_ge(sS, s_qT_done + 2 * (ic - 1))
                            tensor.wait_ge(sV, v_lnq_done + (ic - 1))
                        slot = ic % 8
                        last = None
                        for hc in range(NC):
                            last = tensor.matmul(
                                pg[:], wgr[:, slot * 1024 + hc * 128:slot * 1024 + (hc + 1) * 128],
                                qT[:, hc * TB:(hc + 1) * TB],
                                start=(hc == 0), stop=(hc == NC - 1))
                        inc(last, "sT")
                        for hc in range(NC):
                            last = tensor.matmul(
                                pu[:], wur[:, slot * 1024 + hc * 128:slot * 1024 + (hc + 1) * 128],
                                qT[:, hc * TB:(hc + 1) * TB],
                                start=(hc == 0), stop=(hc == NC - 1))
                        inc(last, "sT")

                @blk_scalar
                def _(scalar):
                    for ic in range(NI):
                        pg = pg0 if ic % 2 == 0 else pg1
                        scalar.wait_ge(sT, t_gu_base + 2 * ic + 1)
                        scalar.activation(Rsb[:], pg[:], AF.Relu)
                        inc(scalar.activation(R2sb[:], Rsb[:], AF.Square), "sS", 2)

                @blk_vector
                def _(vector):
                    for ic in range(NI):
                        pu = pu0 if ic % 2 == 0 else pu1
                        vector.wait_ge(sT, t_gu_base + 2 * ic + 2)
                        vector.wait_ge(sS, s_qT_done + 2 * (ic + 1))
                        vector.tensor_mul(harena[:, ic * TB:(ic + 1) * TB], R2sb[:], pu[:])
                        hsl = harena[:, ic * TB:(ic + 1) * TB]
                        if ic == 0:
                            vector.tensor_tensor(maxrun[:], hsl, hsl, OP.max)
                            inc(vector.tensor_tensor(minrun[:], hsl, hsl, OP.min), "sV")
                        else:
                            vector.tensor_tensor(maxrun[:], hsl, maxrun[:], OP.max)
                            inc(vector.tensor_tensor(minrun[:], hsl, minrun[:], OP.min), "sV")
                v_B_done = C["sV"]

                @blk_vector
                def _(vector):
                    inc(vector.scalar_tensor_tensor(maxrun[:], minrun[:], -1.0, maxrun[:],
                                                    OP.mult, OP.max), "sV")
                v_B_done = C["sV"]

                @blk.tensor
                def _(tensor):
                    tensor.wait_ge(sV, v_B_done)
                    wait_free(tensor, "pt")
                    last = None
                    for k in range(4):
                        last = tensor.transpose(pt[:, k * 128:(k + 1) * 128],
                                                maxrun[:, k * 128:(k + 1) * 128], identb[:])
                    inc(last, "sT")
                t_mx = C["sT"]

                @blk_vector
                def _(vector):
                    vector.wait_ge(sT, t_mx)
                    for k in range(4):
                        vector.reduce_max(lnS[:, 12:13], pt[:, k * 128:(k + 1) * 128], AX.X)
                        vector.tensor_mul(lnS[:, 13:14], acol[:, k:k + 1], acol[:, k:k + 1])
                        vector.tensor_mul(lnS[:, 13:14], lnS[:, 13:14], acol[:, k:k + 1])
                        vector.tensor_scalar(lnS[:, 13:14], lnS[:, 13:14], k1bc[:, 0:1],
                                             None, OP.mult)
                        vector.tensor_mul(lnS[:, 14:15], lnS[:, 13:14], lnS[:, 12:13])
                        vector.tensor_scalar(lnS[:, 14:15], lnS[:, 14:15], EPS, None, OP.max)
                        vector.tensor_scalar(ocols[:, k:k + 1], lnS[:, 14:15], k2bc[:, 0:1],
                                             None, OP.mult)
                        vector.reciprocal(lnS[:, 15:16], lnS[:, 14:15])
                        vector.tensor_mul(lnS[:, 0:1], lnS[:, 14:15], lnS[:, 15:16])
                        vector.tensor_scalar(lnS[:, 0:1], lnS[:, 0:1], -1.0, 2.0,
                                             OP.mult, OP.add)
                        vector.tensor_mul(lnS[:, 15:16], lnS[:, 15:16], lnS[:, 0:1])
                        vector.tensor_scalar(lnS[:, 15:16], lnS[:, 15:16], 127.0, None, OP.mult)
                        vector.tensor_mul(lnS[:, 12:13], lnS[:, 13:14], lnS[:, 15:16])
                        vector.tensor_copy(acol[:, k:k + 1], lnS[:, 12:13])
                    inc(vector.engine_nop(), "sV")
                free["pt"] = ("sV", C["sV"])
                v_scol = C["sV"]

                @blk.tensor
                def _(tensor):
                    tensor.wait_ge(sV, v_scol)
                    wait_free(tensor, "psmall")
                    last = None
                    for k in range(4):
                        last = tensor.matmul(psmall[0:1, k * 128:(k + 1) * 128],
                                             acol[:, k:k + 1], identf[:], start=True, stop=True)
                    inc(last, "sT")
                t_row = C["sT"]

                @blk_vector
                def _(vector):
                    vector.wait_ge(sT, t_row)
                    inc(vector.tensor_copy(sbrow[0:1, :], psmall[0:1, 0:TB]), "sV")
                free["psmall"] = ("sV", C["sV"])
                v_row = C["sV"]

                @blk.tensor
                def _(tensor):
                    tensor.wait_ge(sV, v_row)
                    inc(tensor.matmul(pd0[:], onesr[0:1, :], sbrow[0:1, :],
                                      start=True, stop=True), "sT")
                t_srep = C["sT"]

                @blk_vector
                def _(vector):
                    vector.wait_ge(sT, t_srep)
                    last = None
                    for ic in range(NI):
                        vector.tensor_tensor(m1sb[:], harena[:, ic * TB:(ic + 1) * TB],
                                             pd0[:], OP.mult)
                        vector.tensor_scalar(lnQi[:, 0:TB], m1sb[:], 256.0, None, OP.add)
                        vector.scalar_tensor_tensor(lnCr[:, 0:TB], lnQi[:, 0:TB], -255.5,
                                                    m1sb[:], OP.add, OP.is_le)
                        last = vector.scalar_tensor_tensor(
                            harena[:, ic * TB:(ic + 1) * TB], lnQi[:, 0:TB], -256.0,
                            lnCr[:, 0:TB], OP.add, OP.add)
                    inc(last, "sV")
                v_ih = C["sV"]

                for sp in range(2):
                    wd_tk = []
                    t_dn_pred = C["sT"]

                    @blk.sync
                    def _(sync):
                        for icd in range(NI):
                            if icd - 8 >= 0:
                                sync.wait_ge(sT, t_dn_pred + icd - 8 + 1)
                            elif t_dn_hist:
                                sync.wait_ge(sT, t_dn_hist[-1] + 24 + icd + 1)
                            slot = icd % 8
                            wd_tk.append(QA.issue(sync, [
                                lambda icd=icd, slot=slot: sync.dma_start(
                                    bass.AP(wdr, slot * 1024,
                                            [[8 * 1024, P], [128, NC], [1, P]]),
                                    bass.AP(dswd, icd * P * P,
                                            [[P, P], [NI * P * P, NC], [1, P]]))]))

                    t_dn_base = C["sT"]
                    assert t_dn_base == t_dn_pred
                    t_dn_hist.append(t_dn_base)
                    v_prev_evac = C["sV"]

                    @blk.tensor
                    def _(tensor):
                        tensor.wait_ge(sV, v_prev_evac)
                        for ic in range(NI):
                            QA.wait(tensor, wd_tk[ic])
                            last = None
                            for tc2 in range(2):
                                tc = 2 * sp + tc2
                                for half in range(2):
                                    pp = (pg0, pg1, pu0, pu1)[tc2 * 2 + half]
                                    last = tensor.matmul(
                                        pp[:],
                                        harena[:, ic * TB + tc * 128:ic * TB + (tc + 1) * 128],
                                        wdr[:, (ic % 8) * 1024 + half * 512:(ic % 8) * 1024 + (half + 1) * 512],
                                        start=(ic == 0), stop=(ic == NI - 1))
                            inc(last, "sT")
                    t_dn_done = C["sT"]

                    @blk_vector
                    def _(vector):
                        vector.wait_ge(sT, t_dn_done)
                        last = None
                        for tc2 in range(2):
                            tc = 2 * sp + tc2
                            tile = 4 * b + tc
                            for half in range(2):
                                pp = (pg0, pg1, pu0, pu1)[tc2 * 2 + half]
                                xs2 = x_sb[:, tile * DM + half * 512:tile * DM + (half + 1) * 512]
                                last = vector.scalar_tensor_tensor(
                                    xs2, pp[:], ocols[:, tc:tc + 1], xs2, OP.mult, OP.add)
                        inc(last, "sV")

    def emit_attn():
        v_ffn_done = C["sV"]
        with (
            nc.sbuf_tensor("anorm", [P, NPAIR * S], BF16) as anorm,
            nc.sbuf_tensor("bvrep", [P, DM], F32) as bvrep,
            nc.sbuf_tensor("borep", [P, DM], F32) as borep,
        ):
            with (
                nc.sbuf_tensor("y2T", [P, NC * S], BF16) as y2T,
                nc.sbuf_tensor("vaug", [P, NT * 1040], BF16) as vaug,
            ):

                for tile in range(NT):
                    xsl = x_sb[:, tile * DM:(tile + 1) * DM]

                    @blk_vector
                    def _(vector):
                        if tile == 0:
                            vector.wait_ge(sV, v_ffn_done)
                        vector.reduce_sum(lnS[:, 0:1], xsl, AX.X)
                        inc(vector.tensor_scalar(lnS[:, 1:2], lnS[:, 0:1], -1.0 / DM,
                                                 None, OP.mult), "sV")
                    v_m = C["sV"]

                    @blk_scalar
                    def _(scalar):
                        scalar.wait_ge(sV, v_m)
                        inc(scalar.activation(lnA[:], xsl, AF.Square, bias=lnS[:, 1:2],
                                              accum_out=lnS[:, 2:3]), "sS")
                    s_sq = C["sS"]

                    @blk_vector
                    def _(vector):
                        vector.wait_ge(sS, s_sq)
                        inc(vector.tensor_scalar(lnS[:, 3:4], lnS[:, 2:3], 1.0 / DM, EPS,
                                                 OP.mult, OP.add), "sV")
                    v_t0 = C["sV"]

                    @blk_scalar
                    def _(scalar):
                        scalar.wait_ge(sV, v_t0)
                        inc(scalar.activation(lnS[:, 4:5], lnS[:, 3:4], AF.Sqrt), "sS")
                    s_rt = C["sS"]

                    @blk_vector
                    def _(vector):
                        vector.wait_ge(sS, s_rt)
                        vector.reciprocal(lnS[:, 5:6], lnS[:, 4:5])
                        for _it in range(2):
                            vector.tensor_mul(lnS[:, 6:7], lnS[:, 5:6], lnS[:, 5:6])
                            vector.tensor_mul(lnS[:, 6:7], lnS[:, 6:7], lnS[:, 3:4])
                            vector.tensor_scalar(lnS[:, 6:7], lnS[:, 6:7], -0.5, 1.5,
                                                 OP.mult, OP.add)
                            vector.tensor_mul(lnS[:, 5:6], lnS[:, 5:6], lnS[:, 6:7])
                        vector.tensor_mul(lnS[:, 7:8], lnS[:, 1:2], lnS[:, 5:6])
                        vector.wait_ge(sT, C["sT"])
                        inc(vector.tensor_scalar(q_tok[:], xsl, lnS[:, 5:6], lnS[:, 7:8],
                                                 OP.mult, OP.add), "sV")
                    v_y2 = C["sV"]

                    t_tr_base = C["sT"]
                    s_ev_base = C["sS"]

                    @blk.tensor
                    def _(tensor):
                        tensor.wait_ge(sV, v_y2)
                        for half in range(2):
                            if half == 0:
                                wait_free(tensor, "pt")
                            else:
                                tensor.wait_ge(sS, s_ev_base + 1)
                            last = None
                            for k in range(4):
                                hc = half * 4 + k
                                last = tensor.transpose(pt[:, k * 128:(k + 1) * 128],
                                                        q_tok[:, hc * 128:(hc + 1) * 128],
                                                        identb[:])
                            inc(last, "sT")

                    @blk_scalar
                    def _(scalar):
                        for half in range(2):
                            scalar.wait_ge(sT, t_tr_base + half + 1)
                            hc0 = half * 4
                            inc(scalar.copy(
                                bass.AP(y2T, hc0 * S + tile * 128,
                                        [[NC * S, P], [S, 4], [1, 128]]),
                                bass.AP(pt, 0, [[512, P], [128, 4], [1, 128]])), "sS")
                    free["pt"] = ("sS", C["sS"])

                s_y2T_done = C["sS"]

                with (
                    nc.sbuf_tensor("wslab_v", [P, NC * DM], BF16) as wslab,
                    nc.sbuf_tensor("brow", [1, DM], F32) as brow,
                ):
                    tk_wv = [None]

                    @blk.gpsimd
                    def _(g):
                        g.wait_ge(sV, v_ffn_done)
                        tk_wv[0] = QG.issue(g, [
                            lambda: g.dma_start(
                                bass.AP(wslab, 0, [[NC * DM, P], [DM, NC], [1, DM]]),
                                bass.AP(dWv, 0, [[DM, P], [P * DM, NC], [1, DM]])),
                            lambda: g.dma_start(brow[0:1, 0:DM], dbv.ap())])

                    for h in range(4):
                        if h == 2:
                            t_snap = C["sT"]

                            @blk.gpsimd
                            def _(g):
                                g.wait_ge(sT, t_snap)
                                tk_wv[0] = QG.issue(g, [
                                    lambda: g.dma_start(brow[0:1, 0:DM], dbo.ap())])

                        @blk.tensor
                        def _(tensor):
                            QG.wait(tensor, tk_wv[0])
                            wait_free(tensor, "psmall")
                            inc(tensor.matmul(psmall[:, 0:512], onesr[0:1, :],
                                              brow[0:1, (h % 2) * 512:(h % 2 + 1) * 512],
                                              start=True, stop=True), "sT")
                        t_h = C["sT"]

                        @blk_vector
                        def _(vector):
                            vector.wait_ge(sT, t_h)
                            dst = bvrep if h < 2 else borep
                            inc(vector.tensor_copy(dst[:, (h % 2) * 512:(h % 2 + 1) * 512],
                                                   psmall[:, 0:512]), "sV")
                        free["psmall"] = ("sV", C["sV"])

                    @blk_vector
                    def _(vector):
                        inc(vector.memset(vaug[:], 1.0), "sV")
                    v_ones = C["sV"]

                    t_v_base = C["sT"]

                    @blk.tensor
                    def _(tensor):
                        tensor.wait_ge(sS, s_y2T_done)
                        QG.wait(tensor, tk_wv[0])
                        tensor.wait_ge(sV, v_ones)
                        for tc in range(NT):
                            for half in range(2):
                                psv = pg0 if (2 * tc + half) % 2 == 0 else pg1
                                idx = 2 * tc + half
                                if idx >= 2:
                                    tensor.wait_ge(sV, v_ones + idx - 1)
                                last = None
                                for dc in range(NC):
                                    last = tensor.matmul(
                                        psv[:], y2T[:, dc * S + tc * 128:dc * S + (tc + 1) * 128],
                                        wslab[:, dc * DM + half * 512:dc * DM + (half + 1) * 512],
                                        start=(dc == 0), stop=(dc == NC - 1))
                                inc(last, "sT")

                    @blk_vector
                    def _(vector):
                        for tc in range(NT):
                            for half in range(2):
                                psv = pg0 if (2 * tc + half) % 2 == 0 else pg1
                                idx = 2 * tc + half
                                vector.wait_ge(sT, t_v_base + idx + 1)
                                inc(vector.tensor_add(
                                    bass.AP(vaug, tc * 1040 + half * 520,
                                            [[NT * 1040, P], [65, 8], [1, 64]]),
                                    bass.AP(bvrep, half * 512, [[DM, P], [64, 8], [1, 64]]),
                                    psv[:]), "sV")
                    v_vaug_done = C["sV"]

                with (
                    nc.sbuf_tensor("wqp", [P, NC * 128], BF16) as wqp,
                    nc.sbuf_tensor("wkp", [P, NC * 128], BF16) as wkp,
                    nc.sbuf_tensor("Qp", [P, S], BF16) as Qp,
                    nc.sbuf_tensor("Kp", [P, S], BF16) as Kp,
                    nc.sbuf_tensor("Ebuf", [P, 2048], BF16) as Ebuf,
                    nc.sbuf_tensor("zi", [P, 1024], BF16) as zi,
                    nc.sbuf_tensor("zrA", [P, 512], BF16) as zrA,
                    nc.sbuf_tensor("zrB", [P, 512], BF16) as zrB,
                    nc.sbuf_tensor("nbs", [P, 1024], BF16) as nbs,
                ):
                    nb_dma_hist = []
                    for c in range(NPAIR):
                        tk_qk = [None]

                        @blk.gpsimd
                        def _(g):
                            if c == 0:
                                g.wait_ge(sV, v_vaug_done)
                            else:
                                g.wait_ge(sV, C["sV"])
                            tk_qk[0] = QG.issue(g, [
                                lambda: g.dma_start(
                                    bass.AP(wqp, 0, [[NC * 128, P], [128, NC], [1, 128]]),
                                    bass.AP(dWq, 128 * c, [[DM, P], [P * DM, NC], [1, 128]])),
                                lambda: g.dma_start(
                                    bass.AP(wkp, 0, [[NC * 128, P], [128, NC], [1, 128]]),
                                    bass.AP(dWk, 128 * c, [[DM, P], [P * DM, NC], [1, 128]]))])

                        t_qk_base = C["sT"]
                        s_qk_pre = C["sS"]

                        @blk.tensor
                        def _(tensor):
                            QG.wait(tensor, tk_qk[0])
                            tensor.wait_ge(sV, C["sV"] if c > 0 else v_vaug_done)
                            for s4 in range(4):
                                if s4 >= 1:
                                    tensor.wait_ge(sS, s_qk_pre + 2 * s4)
                                last = None
                                for dc in range(NC):
                                    last = tensor.matmul(
                                        pg0[:], wqp[:, dc * 128:(dc + 1) * 128],
                                        y2T[:, dc * S + s4 * 512:dc * S + (s4 + 1) * 512],
                                        start=(dc == 0), stop=(dc == NC - 1))
                                inc(last, "sT")
                                for dc in range(NC):
                                    last = tensor.matmul(
                                        pg1[:], wkp[:, dc * 128:(dc + 1) * 128],
                                        y2T[:, dc * S + s4 * 512:dc * S + (s4 + 1) * 512],
                                        start=(dc == 0), stop=(dc == NC - 1))
                                inc(last, "sT")

                        @blk_scalar
                        def _(scalar):
                            for s4 in range(4):
                                scalar.wait_ge(sT, t_qk_base + 2 * s4 + 1)
                                scalar.activation(Qp[:, s4 * 512:(s4 + 1) * 512], pg0[:],
                                                  AF.Identity, bias=bcols[:, c:c + 1],
                                                  scale=f32(1.0))
                                scalar.wait_ge(sT, t_qk_base + 2 * s4 + 2)
                                inc(scalar.activation(Kp[:, s4 * 512:(s4 + 1) * 512], pg1[:],
                                                      AF.Identity, bias=bcols[:, NC + c:NC + c + 1],
                                                      scale=f32(1.0)), "sS", 2)
                        s_qk = C["sS"]

                        for s4 in range(4):
                            kcs = list(range(4 * s4 + 4))
                            nkc = len(kcs)
                            t_core_base = C["sT"]
                            s_core_base = C["sS"]
                            v_core_base = C["sV"]

                            @blk.tensor
                            def _(tensor):
                                tensor.wait_ge(sS, s_qk)
                                for i4, kc in enumerate(kcs):
                                    o = max(0, 128 * kc - 512 * s4)
                                    if i4 >= 1:
                                        tensor.wait_ge(sS, s_core_base + 2 * i4)
                                    tensor.matmul(pg0[:, o:512],
                                                  Kp[0:64, kc * 128:(kc + 1) * 128],
                                                  Qp[0:64, s4 * 512 + o:(s4 + 1) * 512],
                                                  start=True, stop=True)
                                    inc(tensor.matmul(pg1[:, o:512],
                                                      Kp[64:128, kc * 128:(kc + 1) * 128],
                                                      Qp[64:128, s4 * 512 + o:(s4 + 1) * 512],
                                                      start=True, stop=True), "sT")
                                    tensor.wait_ge(sV, v_core_base + i4 + 1)
                                    eb = (i4 % 2) * 1024
                                    tensor.matmul(pu0[0:65, o:512],
                                                  bass.AP(vaug, kc * 1040 + 65 * (2 * c),
                                                          [[NT * 1040, P], [1, 65]]),
                                                  Ebuf[:, eb + o:eb + 512],
                                                  start=(i4 == 0), stop=(i4 == nkc - 1))
                                    inc(tensor.matmul(pu1[0:65, o:512],
                                                      bass.AP(vaug, kc * 1040 + 65 * (2 * c + 1),
                                                              [[NT * 1040, P], [1, 65]]),
                                                      Ebuf[:, eb + 512 + o:eb + 1024],
                                                      start=(i4 == 0), stop=(i4 == nkc - 1)),
                                        "sT")
                            t_pv_done = C["sT"]

                            @blk_scalar
                            def _(scalar):
                                for i4, kc in enumerate(kcs):
                                    o = max(0, 128 * kc - 512 * s4)
                                    eb = (i4 % 2) * 1024
                                    scalar.wait_ge(sT, t_core_base + 2 * i4 + 1)
                                    if i4 >= 2:
                                        scalar.wait_ge(sT, t_core_base + 2 * (i4 - 1))
                                    scalar.activation(Ebuf[:, eb + o:eb + 512], pg0[:, o:512],
                                                      AF.Exp, scale=f32(0.125), bias=f32(0.0))
                                    inc(scalar.activation(Ebuf[:, eb + 512 + o:eb + 1024],
                                                          pg1[:, o:512], AF.Exp,
                                                          scale=f32(0.125), bias=f32(0.0)),
                                        "sS", 2)

                            @blk_vector
                            def _(vector):
                                for i4, kc in enumerate(kcs):
                                    o = max(0, 128 * kc - 512 * s4)
                                    eb = (i4 % 2) * 1024
                                    vector.wait_ge(sS, s_core_base + 2 * (i4 + 1))
                                    if kc >= 4 * s4:
                                        vector.tensor_mul(Ebuf[:, eb + o:eb + o + 128],
                                                          Ebuf[:, eb + o:eb + o + 128], tri[:])
                                        vector.tensor_mul(
                                            Ebuf[:, eb + 512 + o:eb + 512 + o + 128],
                                            Ebuf[:, eb + 512 + o:eb + 512 + o + 128], tri[:])
                                    inc(vector.engine_nop(), "sV")

                            @blk_vector
                            def _(vector):
                                vector.wait_ge(sT, t_pv_done)
                                with nc.allow_low_precision(reason="1/Z as bf16 matmul operand"):
                                    vector.reciprocal(zi[64:65, 0:512], pu0[64:65, :])
                                    inc(vector.reciprocal(zi[64:65, 512:1024], pu1[64:65, :]),
                                        "sV")
                            v_zi = C["sV"]

                            @blk.tensor
                            def _(tensor):
                                tensor.wait_ge(sV, v_zi)
                                tensor.matmul(pd0[0:64, :], onesbb[64:65, :],
                                              zi[64:65, 0:512], start=True, stop=True)
                                inc(tensor.matmul(pd1[0:64, :], onesbb[64:65, :],
                                                  zi[64:65, 512:1024], start=True, stop=True),
                                    "sT")
                            t_z = C["sT"]

                            nbh = (4 * c + s4) % 2

                            @blk_scalar
                            def _(scalar):
                                scalar.wait_ge(sT, t_z)
                                scalar.copy(zrA[0:64, :], pd0[0:64, :])
                                inc(scalar.copy(zrB[0:64, :], pd1[0:64, :]), "sS")
                            s_zr = C["sS"]

                            @blk_vector
                            def _(vector):
                                vector.wait_ge(sS, s_zr)
                                vector.tensor_mul(
                                    anorm[0:64, c * S + s4 * 512:c * S + (s4 + 1) * 512],
                                    pu0[0:64, :], zrA[0:64, :])
                                if len(nb_dma_hist) >= 2:
                                    QG.wait(vector, nb_dma_hist[-2])
                                inc(vector.tensor_mul(nbs[0:64, nbh * 512:(nbh + 1) * 512],
                                                      pu1[0:64, :], zrB[0:64, :]), "sV")
                            v_nb = C["sV"]

                            @blk.gpsimd
                            def _(g):
                                g.wait_ge(sV, v_nb)
                                nb_dma_hist.append(QG.issue(g, [lambda: g.dma_start(
                                    anorm[64:128, c * S + s4 * 512:c * S + (s4 + 1) * 512],
                                    nbs[0:64, nbh * 512:(nbh + 1) * 512])]))

            v_pair_done = C["sV"]
            with nc.sbuf_tensor("wslab_o", [P, NC * DM], BF16) as wslab_o:
                tk_wo = [None]

                @blk.gpsimd
                def _(g):
                    g.wait_ge(sV, v_pair_done)
                    tk_wo[0] = QG.issue(g, [lambda: g.dma_start(
                        bass.AP(wslab_o, 0, [[NC * DM, P], [DM, NC], [1, DM]]),
                        bass.AP(dWo, 0, [[DM, P], [P * DM, NC], [1, DM]]))])

                t_op_base = C["sT"]

                @blk.tensor
                def _(tensor):
                    QG.wait_all(tensor)
                    tensor.wait_ge(sV, v_pair_done)
                    for tc in range(NT):
                        for half in range(2):
                            pdx = pd0 if (2 * tc + half) % 2 == 0 else pd1
                            idx = 2 * tc + half
                            if idx >= 2:
                                tensor.wait_ge(sV, v_pair_done + idx - 1)
                            last = None
                            for cc in range(NPAIR):
                                last = tensor.matmul(
                                    pdx[:], anorm[:, cc * S + tc * 128:cc * S + (tc + 1) * 128],
                                    wslab_o[:, cc * DM + half * 512:cc * DM + (half + 1) * 512],
                                    start=(cc == 0), stop=(cc == NPAIR - 1))
                            inc(last, "sT")

                @blk_vector
                def _(vector):
                    for tc in range(NT):
                        for half in range(2):
                            pdx = pd0 if (2 * tc + half) % 2 == 0 else pd1
                            idx = 2 * tc + half
                            vector.wait_ge(sT, t_op_base + idx + 1)
                            xs2 = x_sb[:, tc * DM + half * 512:tc * DM + (half + 1) * 512]
                            vector.tensor_add(m1sb[:], pdx[:],
                                              borep[:, half * 512:(half + 1) * 512])
                            inc(vector.tensor_add(xs2, xs2, m1sb[:]), "sV")

    emit_ffn(0)
    emit_attn()
    emit_ffn(1)

    v_all = C["sV"]

    @blk.sync
    def _(sync):
        sync.wait_ge(sV, v_all)
        tko = QA.issue(sync, [lambda: sync.dma_start(
            bass.AP(dout, 0, [[DM, P], [P * DM, NT], [1, DM]]),
            bass.AP(x_sb, 0, [[NT * DM, P], [DM, NT], [1, DM]]))])
        QA.wait(sync, tko)
        QG.wait_all(sync)

    es.close()
    return nc


_NC_CACHE = None


def kernel(L0_attn_x, L0_attn_Wq, L0_attn_bq, L0_attn_Wk, L0_attn_bk,
           L0_attn_Wv, L0_attn_bv, L0_attn_Wo, L0_attn_bo,
           L0_ffn_w_gate, L0_ffn_w_up, L0_ffn_w_down):
    global _NC_CACHE
    if _NC_CACHE is None:
        _NC_CACHE = build_core()
    nc = _NC_CACHE

    x = np.ascontiguousarray(np.asarray(L0_attn_x, dtype=np.float32))
    shared = dict(
        Wq=np.ascontiguousarray(np.asarray(L0_attn_Wq, dtype=np.float32)),
        Wk=np.ascontiguousarray(np.asarray(L0_attn_Wk, dtype=np.float32)),
        Wv=np.ascontiguousarray(np.asarray(L0_attn_Wv, dtype=np.float32)),
        Wo=np.ascontiguousarray(np.asarray(L0_attn_Wo, dtype=np.float32)),
        bq=np.ascontiguousarray(np.asarray(L0_attn_bq, dtype=np.float32).reshape(1, DM)),
        bk=np.ascontiguousarray(np.asarray(L0_attn_bk, dtype=np.float32).reshape(1, DM)),
        bv=np.ascontiguousarray(np.asarray(L0_attn_bv, dtype=np.float32).reshape(1, DM)),
        bo=np.ascontiguousarray(np.asarray(L0_attn_bo, dtype=np.float32).reshape(1, DM)),
        wg=np.ascontiguousarray(np.asarray(L0_ffn_w_gate, dtype=np.float32)),
        wu=np.ascontiguousarray(np.asarray(L0_ffn_w_up, dtype=np.float32)),
        wd=np.ascontiguousarray(np.asarray(L0_ffn_w_down, dtype=np.float32)),
        idb=np.eye(P, dtype=ml_dtypes.bfloat16),
        idf=np.eye(P, dtype=np.float32),
        tri=np.triu(np.ones((P, P), dtype=ml_dtypes.bfloat16)),
        onesf=np.ones((1, P), dtype=np.float32),
        onesb=np.ones((1, P), dtype=ml_dtypes.bfloat16),
    )
    in_maps = [dict(x=np.ascontiguousarray(x[i]), **shared) for i in range(8)]
    res = run_bass_kernel_spmd(nc, in_maps, core_ids=list(range(8)))
    out = np.stack([np.asarray(r["out"], dtype=np.float32) for r in res.results], axis=0)
    return out


# version 05 (final)
# speedup vs baseline: 1.0223x; 1.0134x over previous
import numpy as np
import ml_dtypes
from contextlib import ExitStack

import concourse.bass as bass
import concourse.mybir as mybir
from concourse.bass_utils import run_bass_kernel_spmd

F32 = mybir.dt.float32
BF16 = mybir.dt.bfloat16
I32 = mybir.dt.int32
AF = mybir.ActivationFunctionType
OP = mybir.AluOpType
AX = mybir.AxisListType
f32 = np.float32

P = 128
S = 2048
NT = 16
DM = 1024
NC = 8
II = 4096
NI = 32
TB = 512
NB = 4
NPAIR = 8
EPS = 1e-5


_VOPS = ("tensor_scalar", "tensor_tensor", "tensor_add", "tensor_mul",
         "tensor_copy", "reduce_sum", "reduce_max", "reciprocal", "memset",
         "scalar_tensor_tensor", "engine_nop")


class _PubInstr:

    def __init__(self, eng, instr):
        self._eng = eng
        self._i = instr

    def then_inc(self, sem, n):
        d = self._eng.drain()
        d.then_inc(sem, n)
        return d


def _ap_name(a):
    try:
        return a.tensor.name
    except AttributeError:
        try:
            return a.name
        except AttributeError:
            return None


class _DV:

    _recent = []
    _seen = set()

    def __init__(self, eng):
        self._e = eng

    def __getattr__(self, name):
        attr = getattr(self._e, name)
        if name in _VOPS:
            e = self._e

            def wrapped(*a, **k):
                out = _ap_name(a[0]) if a else None
                ins = [n for n in (_ap_name(x) for x in a[1:]) if n]
                need = (out is not None and out not in _DV._seen) or \
                       any(n in _DV._recent for n in ins) or out in _DV._recent
                if need:
                    e.drain()
                    _DV._recent.clear()
                if out is not None:
                    _DV._seen.add(out)
                    _DV._recent.append(out)
                    if len(_DV._recent) > 3:
                        _DV._recent.pop(0)
                return attr(*a, **k)
            return wrapped
        return attr


_SOPS = ("activation", "copy")


class _DS:

    def __init__(self, eng):
        self._e = eng

    def __getattr__(self, name):
        attr = getattr(self._e, name)
        if name in _SOPS:
            e = self._e

            def wrapped(*a, **k):
                return _PubInstr(e, attr(*a, **k))
            return wrapped
        return attr


def build_core():
    _DV._recent = []
    _DV._seen = set()
    nc = bass.Bass("TRN2", target_bir_lowering=False, detect_race_conditions=False)
    es = ExitStack()
    C = {"sdA": 0, "sdG": 0, "sT": 0, "sV": 0, "sS": 0}

    dx = nc.dram_tensor("x", [S, DM], F32, kind="ExternalInput")
    dWq = nc.dram_tensor("Wq", [DM, DM], F32, kind="ExternalInput")
    dWk = nc.dram_tensor("Wk", [DM, DM], F32, kind="ExternalInput")
    dWv = nc.dram_tensor("Wv", [DM, DM], F32, kind="ExternalInput")
    dWo = nc.dram_tensor("Wo", [DM, DM], F32, kind="ExternalInput")
    dbq = nc.dram_tensor("bq", [1, DM], F32, kind="ExternalInput")
    dbk = nc.dram_tensor("bk", [1, DM], F32, kind="ExternalInput")
    dbv = nc.dram_tensor("bv", [1, DM], F32, kind="ExternalInput")
    dbo = nc.dram_tensor("bo", [1, DM], F32, kind="ExternalInput")
    dwg = nc.dram_tensor("wg", [II, DM], F32, kind="ExternalInput")
    dwu = nc.dram_tensor("wu", [II, DM], F32, kind="ExternalInput")
    dwd = nc.dram_tensor("wd", [DM, II], F32, kind="ExternalInput")
    didb = nc.dram_tensor("idb", [P, P], BF16, kind="ExternalInput")
    didf = nc.dram_tensor("idf", [P, P], F32, kind="ExternalInput")
    dtri = nc.dram_tensor("tri", [P, P], BF16, kind="ExternalInput")
    dones = nc.dram_tensor("onesf", [1, P], F32, kind="ExternalInput")
    donesb = nc.dram_tensor("onesb", [1, P], BF16, kind="ExternalInput")
    dout = nc.dram_tensor("out", [S, DM], F32, kind="ExternalOutput")
    dswg = nc.dram_tensor("swg", [II * NC, P], BF16, kind="Internal")
    dswu = nc.dram_tensor("swu", [II * NC, P], BF16, kind="Internal")
    dswd = nc.dram_tensor("swd", [II * NC, P], BF16, kind="Internal")

    def sb(name, shape, dt):
        return es.enter_context(nc.sbuf_tensor(name, shape, dt))

    def ps_(name, shape, dt):
        return es.enter_context(nc.psum_tensor(name, shape, dt))

    identb = sb("identb", [P, P], BF16)
    identf = sb("identf", [P, P], F32)
    tri = sb("tri_sb", [P, P], BF16)
    onesr = sb("onesr", [P, P], F32)
    onesbb = sb("onesbb", [P, 64], BF16)
    bcols = sb("bcols", [P, 4 * NC], F32)
    mw = sb("mw", [1, 8], F32)
    k1bc = sb("k1bc", [P, 1], F32)
    k2bc = sb("k2bc", [P, 1], F32)
    onescol = sb("onescol", [P, 1], F32)

    pg0 = ps_("pg0", [P, 512], F32)
    pg1 = ps_("pg1", [P, 512], F32)
    pu0 = ps_("pu0", [P, 512], F32)
    pu1 = ps_("pu1", [P, 512], F32)
    pd0 = ps_("pd0", [P, 512], F32)
    pd1 = ps_("pd1", [P, 512], F32)
    psmall = ps_("psmall", [P, 512], F32)
    pt = ps_("pt", [P, 512], BF16)

    sdA = es.enter_context(nc.semaphore("sdA"))
    sdG = es.enter_context(nc.semaphore("sdG"))
    sT = es.enter_context(nc.semaphore("sT"))
    sV = es.enter_context(nc.semaphore("sV"))
    sS = es.enter_context(nc.semaphore("sS"))
    SEM = {"sdA": sdA, "sdG": sdG, "sT": sT, "sV": sV, "sS": sS}

    def inc(instr, s, n=1):
        C[s] += n
        instr.then_inc(SEM[s], n)
        return C[s]

    class DmaQ:

        def __init__(self, name, n=4):
            self.sems = [es.enter_context(nc.semaphore(f"{name}{i}")) for i in range(n)]
            self.tot = [0] * n
            self.i = 0

        def issue(self, eng, fns):
            j = self.i % len(self.sems)
            self.i += 1
            eng.wait_ge(self.sems[j], self.tot[j])
            for fn in fns:
                instr = fn()
                self.tot[j] += 16
                instr.then_inc(self.sems[j], 16)
            return (j, self.tot[j])

        def wait(self, eng, ticket):
            eng.wait_ge(self.sems[ticket[0]], ticket[1])

        def wait_all(self, eng):
            for j, s in enumerate(self.sems):
                eng.wait_ge(s, self.tot[j])

    QA = DmaQ("qa")
    QG = DmaQ("qg")

    free = {"pt": None, "psmall": None}

    def wait_free(eng, key):
        if free[key] is not None:
            eng.wait_ge(SEM[free[key][0]], free[key][1])

    blk = es.enter_context(nc.Block())

    _real_vec_dec = blk.vector
    _real_sca_dec = blk.scalar

    def vecblk(fn):
        return _real_vec_dec(lambda vector: fn(_DV(vector)))
    blk_vector = vecblk

    def scablk(fn):
        return _real_sca_dec(fn)
    blk_scalar = scablk

    tk_consts = [None]

    @blk.sync
    def _(sync):
        fns = [lambda d=dst, s=srcd: sync.dma_start(d[:], s.ap())
               for dst, srcd in ((identb, didb), (identf, didf), (tri, dtri))]
        fns.append(lambda: sync.dma_start(onesr[0:1, :], dones.ap()))
        fns.append(lambda: sync.dma_start(onesbb[64:65, :],
                                          bass.AP(donesb, 0, [[1, 1], [1, 64]])))
        for i4, db in enumerate((dbq, dbk, dbv, dbo)):
            fns.append(lambda i4=i4, db=db: sync.dma_start(
                bass.AP(bcols, 8 * i4, [[4 * NC, P], [1, NC]]),
                bass.AP(db, 0, [[1, P], [128, NC]]),
                allow_slow_non_contiguous=True))
        tk_consts[0] = QA.issue(sync, fns)

    @blk_vector
    def _(vector):
        QA.wait(vector, tk_consts[0])
        inc(vector.memset(onescol[:], 1.0), "sV")

    @blk.tensor
    def _(tensor):
        QA.wait(tensor, tk_consts[0])

    @blk_scalar
    def _(scalar):
        QA.wait(scalar, tk_consts[0])

    def quant_weight(widx, dsrc, dscr, is_wd):
        with (
            nc.sbuf_tensor(f"W32_{widx}", [P, 32768], F32) as W32,
            nc.sbuf_tensor(f"sgA_{widx}", [P, 4096], F32) as sgA,
            nc.sbuf_tensor(f"sgB_{widx}", [P, 4096], F32) as sgB,
            nc.sbuf_tensor(f"tq_{widx}", [P, 8192], BF16) as tq,
            nc.sbuf_tensor(f"st2_{widx}", [P, 8192], BF16) as st2,
            nc.sbuf_tensor(f"prt_{widx}", [P, 10], F32) as prt,
            nc.sbuf_tensor(f"acc8_{widx}", [P, 8], F32) as acc8,
            nc.sbuf_tensor(f"sbc_{widx}", [P, 1], F32) as sbc,
        ):
            vw = C["sV"]
            sw = C["sS"]
            tw = C["sT"]
            w32_tk = []

            @blk.sync
            def _(sync):
                sync.wait_ge(sV, vw)
                sync.wait_ge(sS, sw)
                sync.wait_ge(sT, tw)
                QG.wait_all(sync)
                for j in range(4):
                    if is_wd:
                        fn = lambda j=j: sync.dma_start(
                            bass.AP(W32, j * 8192, [[32768, P], [4096, 2], [1, 4096]]),
                            bass.AP(dsrc, j * 2 * P * II, [[II, P], [P * II, 2], [1, II]]))
                    else:
                        fn = lambda j=j: sync.dma_start(
                            bass.AP(W32, j * 8192, [[32768, P], [1024, 8], [1, 1024]]),
                            bass.AP(dsrc, j * 8 * P * DM, [[DM, P], [P * DM, 8], [1, DM]]))
                    w32_tk.append(QA.issue(sync, [fn]))

            @blk_vector
            def _(vector):
                for j in range(4):
                    QA.wait(vector, w32_tk[j])
                    vector.reduce_sum(prt[:, 2 * j:2 * j + 1],
                                      W32[:, j * 8192:j * 8192 + 4096],
                                      AX.X, apply_absolute_value=True)
                    r = vector.reduce_sum(prt[:, 2 * j + 1:2 * j + 2],
                                          W32[:, j * 8192 + 4096:(j + 1) * 8192],
                                          AX.X, apply_absolute_value=True)
                    if j == 3:
                        inc(r, "sV")
            v_sum = C["sV"]

            @blk_scalar
            def _(scalar):
                scalar.wait_ge(sV, v_sum)
                inc(scalar.activation(acc8[:], prt[:, 0:8], AF.Identity,
                                      accum_out=prt[:, 8:9]), "sS")
            s_tot8 = C["sS"]

            @blk.tensor
            def _(tensor):
                tensor.wait_ge(sS, s_tot8)
                wait_free(tensor, "psmall")
                inc(tensor.matmul(psmall[0:1, 0:1], prt[:, 8:9], onescol[:, 0:1],
                                  start=True, stop=True), "sT")
            t_tot = C["sT"]

            @blk_vector
            def _(vector):
                vector.wait_ge(sT, t_tot)
                inc(vector.tensor_scalar(mw[0:1, widx:widx + 1], psmall[0:1, 0:1],
                                         1.0 / 4194304.0, EPS, OP.mult, OP.max), "sV")
            v_mw = C["sV"]

            @blk_scalar
            def _(scalar):
                scalar.wait_ge(sV, v_mw)
                inc(scalar.copy(mw[0:1, 6:7], mw[0:1, widx:widx + 1]), "sS")
            s_mw = C["sS"]

            @blk_vector
            def _(vector):
                vector.wait_ge(sS, s_mw)
                vector.reciprocal(mw[0:1, 7:8], mw[0:1, 6:7])
                vector.tensor_mul(mw[0:1, 5:6], mw[0:1, 6:7], mw[0:1, 7:8])
                vector.tensor_scalar(mw[0:1, 5:6], mw[0:1, 5:6], -1.0, 2.0,
                                     OP.mult, OP.add)
                inc(vector.tensor_mul(mw[0:1, 7:8], mw[0:1, 7:8], mw[0:1, 5:6]), "sV")
            v_m = C["sV"]

            @blk.tensor
            def _(tensor):
                tensor.wait_ge(sV, v_m)
                inc(tensor.matmul(psmall[:, 0:1], onesr[0:1, :], mw[0:1, 7:8],
                                  start=True, stop=True), "sT")
            t_bc = C["sT"]

            @blk_vector
            def _(vector):
                vector.wait_ge(sT, t_bc)
                inc(vector.tensor_copy(sbc[:], psmall[:, 0:1]), "sV")
            v_sbc = C["sV"]
            free["psmall"] = ("sV", v_sbc)

            tq_tdone = [None] * 8
            st2_gdone = [None] * 8
            for j in range(8):
                jb = j % 2

                @blk_scalar
                def _(scalar):
                    scalar.wait_ge(sV, v_sbc)
                    if j >= 1:
                        scalar.wait_ge(sV, C["sV"])
                    scalar.activation(sgA[:], W32[:, j * 4096:(j + 1) * 4096], AF.Sign,
                                      bias=f32(-0.5), scale=sbc[:, 0:1])
                    inc(scalar.activation(sgB[:], W32[:, j * 4096:(j + 1) * 4096], AF.Sign,
                                          bias=f32(0.5), scale=sbc[:, 0:1]), "sS")
                s_sg = C["sS"]

                @blk_vector
                def _(vector):
                    vector.wait_ge(sS, s_sg)
                    if j >= 2 and tq_tdone[j - 2] is not None:
                        vector.wait_ge(sT, tq_tdone[j - 2])
                    inc(vector.tensor_add(tq[:, jb * 4096:(jb + 1) * 4096], sgA[:], sgB[:]), "sV")
                v_tq = C["sV"]

                t_tr_base = C["sT"]

                @blk.tensor
                def _(tensor):
                    tensor.wait_ge(sV, v_tq)
                    for bat in range(8):
                        if bat == 0:
                            wait_free(tensor, "pt")
                        else:
                            tensor.wait_ge(sS, s_sg + bat)
                        last = None
                        for k in range(4):
                            bi = bat * 4 + k
                            last = tensor.transpose(
                                pt[:, k * 128:(k + 1) * 128],
                                tq[:, jb * 4096 + bi * 128:jb * 4096 + (bi + 1) * 128],
                                identb[:])
                        inc(last, "sT")
                tq_tdone[j] = C["sT"]

                @blk_scalar
                def _(scalar):
                    for bat in range(8):
                        scalar.wait_ge(sT, t_tr_base + bat + 1)
                        if j >= 2 and bat == 0:
                            QG.wait(scalar, st2_gdone[j - 2])
                        inc(scalar.copy(st2[:, jb * 4096 + bat * 512:jb * 4096 + (bat + 1) * 512],
                                        pt[:, 0:512]), "sS")
                s_evb = C["sS"]
                free["pt"] = ("sS", s_evb)

                @blk.gpsimd
                def _(g):
                    g.wait_ge(sS, s_evb)
                    st2_gdone[j] = QG.issue(g, [lambda: g.dma_start(
                        bass.AP(dscr, (j * 32) * 128 * P, [[P, P], [128 * P, 32], [1, P]]),
                        bass.AP(st2, jb * 4096, [[8192, P], [128, 32], [1, P]]))])

    quant_weight(0, dwg, dswg, False)
    quant_weight(1, dwu, dswu, False)
    quant_weight(2, dwd, dswd, True)

    x_sb = sb("x_sb", [P, NT * DM], F32)
    lnA = sb("lnA", [P, DM], F32)
    lnY = sb("lnY", [P, DM], F32)
    lnS = sb("lnS", [P, 16], F32)
    acol = sb("acol", [P, 4], F32)
    q_tok = sb("q_tok", [P, DM], BF16)
    m1sb = sb("m1sb", [P, 512], F32)
    sbrow = sb("sbrow", [1, TB], F32)
    ocols = sb("ocols", [P, 4], F32)

    @blk_vector
    def _(vector):
        vector.tensor_mul(mw[0:1, 3:4], mw[0:1, 0:1], mw[0:1, 0:1])
        vector.tensor_mul(mw[0:1, 3:4], mw[0:1, 3:4], mw[0:1, 1:2])
        vector.tensor_scalar(mw[0:1, 3:4], mw[0:1, 3:4], 1.0 / (127.0 ** 3 * 8.0), None, OP.mult)
        inc(vector.tensor_scalar(mw[0:1, 4:5], mw[0:1, 2:3], 0.5 / (127.0 * 2.0), None, OP.mult), "sV")
    v_k = C["sV"]

    @blk.tensor
    def _(tensor):
        tensor.wait_ge(sV, v_k)
        wait_free(tensor, "psmall")
        tensor.matmul(psmall[:, 0:1], onesr[0:1, :], mw[0:1, 3:4], start=True, stop=True)
        inc(tensor.matmul(psmall[:, 1:2], onesr[0:1, :], mw[0:1, 4:5], start=True, stop=True), "sT")
    t_k = C["sT"]

    @blk_vector
    def _(vector):
        vector.wait_ge(sT, t_k)
        vector.tensor_copy(k1bc[:], psmall[:, 0:1])
        inc(vector.tensor_copy(k2bc[:], psmall[:, 1:2]), "sV")
    free["psmall"] = ("sV", C["sV"])

    tk_x = [None]

    vq_done, sq_done, tq_done = C["sV"], C["sS"], C["sT"]

    @blk.sync
    def _(sync):
        sync.wait_ge(sV, vq_done)
        sync.wait_ge(sS, sq_done)
        sync.wait_ge(sT, tq_done)
        QG.wait_all(sync)
        tk_x[0] = QA.issue(sync, [lambda: sync.dma_start(
            bass.AP(x_sb, 0, [[NT * DM, P], [DM, NT], [1, DM]]),
            bass.AP(dx, 0, [[DM, P], [P * DM, NT], [1, DM]]))])

    def emit_ffn(call):
        with (
            nc.sbuf_tensor(f"wgr{call}", [P, 8 * 1024], BF16) as wgr,
            nc.sbuf_tensor(f"wur{call}", [P, 8 * 1024], BF16) as wur,
            nc.sbuf_tensor(f"wdr{call}", [P, 8 * 1024], BF16) as wdr,
            nc.sbuf_tensor(f"qT{call}", [P, NC * TB], BF16) as qT,
            nc.sbuf_tensor(f"harena{call}", [P, NI * TB], BF16) as harena,
            nc.sbuf_tensor(f"maxrun{call}", [P, TB], BF16) as maxrun,
            nc.sbuf_tensor(f"minrun{call}", [P, TB], BF16) as minrun,
            nc.sbuf_tensor(f"lnQi{call}", [P, DM], I32) as lnQi,
            nc.sbuf_tensor(f"lnCr{call}", [P, DM], F32) as lnCr,
            nc.sbuf_tensor(f"Rsb{call}", [P, 512], F32) as Rsb,
            nc.sbuf_tensor(f"R2sb{call}", [P, 512], F32) as R2sb,
        ):
            @blk.sync
            def _(sync):
                sync.wait_ge(sV, C["sV"])
                sync.wait_ge(sS, C["sS"])
                sync.wait_ge(sT, C["sT"])
                if call == 0:
                    QG.wait_all(sync)

            t_gu_hist = []
            t_dn_hist = []

            for b in range(NB):
                for j in range(4):
                    tile = 4 * b + j
                    xsl = x_sb[:, tile * DM:(tile + 1) * DM]

                    @blk_vector
                    def _(vector):
                        if call == 0 and b == 0 and j == 0:
                            QA.wait(vector, tk_x[0])
                        vector.reduce_sum(lnS[:, 0:1], xsl, AX.X)
                        inc(vector.tensor_scalar(lnS[:, 1:2], lnS[:, 0:1], -1.0 / DM,
                                                 None, OP.mult), "sV")
                    v_m = C["sV"]

                    @blk_scalar
                    def _(scalar):
                        scalar.wait_ge(sV, v_m)
                        inc(scalar.activation(lnA[:], xsl, AF.Square, bias=lnS[:, 1:2],
                                              accum_out=lnS[:, 2:3]), "sS")
                    s_sq = C["sS"]

                    @blk_vector
                    def _(vector):
                        vector.wait_ge(sS, s_sq)
                        inc(vector.tensor_scalar(lnS[:, 3:4], lnS[:, 2:3], 1.0 / DM, EPS,
                                                 OP.mult, OP.add), "sV")
                    v_t0 = C["sV"]

                    @blk_scalar
                    def _(scalar):
                        scalar.wait_ge(sV, v_t0)
                        inc(scalar.activation(lnS[:, 4:5], lnS[:, 3:4], AF.Sqrt), "sS")
                    s_rt = C["sS"]

                    @blk_vector
                    def _(vector):
                        vector.wait_ge(sS, s_rt)
                        vector.reciprocal(lnS[:, 5:6], lnS[:, 4:5])
                        for _it in range(2):
                            vector.tensor_mul(lnS[:, 6:7], lnS[:, 5:6], lnS[:, 5:6])
                            vector.tensor_mul(lnS[:, 6:7], lnS[:, 6:7], lnS[:, 3:4])
                            vector.tensor_scalar(lnS[:, 6:7], lnS[:, 6:7], -0.5, 1.5,
                                                 OP.mult, OP.add)
                            vector.tensor_mul(lnS[:, 5:6], lnS[:, 5:6], lnS[:, 6:7])
                        vector.tensor_mul(lnS[:, 7:8], lnS[:, 1:2], lnS[:, 5:6])
                        vector.tensor_scalar(lnY[:], xsl, lnS[:, 5:6], lnS[:, 7:8],
                                             OP.mult, OP.add)
                        vector.reduce_max(lnS[:, 8:9], lnY[:], AX.X, apply_absolute_value=True)
                        vector.tensor_scalar(lnS[:, 9:10], lnS[:, 8:9], EPS, None, OP.max)
                        vector.reciprocal(lnS[:, 10:11], lnS[:, 9:10])
                        vector.tensor_mul(lnS[:, 12:13], lnS[:, 9:10], lnS[:, 10:11])
                        vector.tensor_scalar(lnS[:, 12:13], lnS[:, 12:13], -1.0, 2.0,
                                             OP.mult, OP.add)
                        vector.tensor_mul(lnS[:, 10:11], lnS[:, 10:11], lnS[:, 12:13])
                        vector.tensor_scalar(lnS[:, 11:12], lnS[:, 10:11], 127.0, None, OP.mult)
                        vector.tensor_copy(acol[:, j:j + 1], lnS[:, 9:10])
                        vector.tensor_scalar(lnA[:], lnY[:], lnS[:, 11:12], 256.0, OP.mult, OP.add)
                        vector.tensor_copy(lnQi[:], lnA[:])
                        vector.scalar_tensor_tensor(lnCr[:], lnQi[:], -255.5, lnA[:],
                                                    OP.add, OP.is_le)
                        vector.wait_ge(sT, C["sT"])
                        inc(vector.scalar_tensor_tensor(q_tok[:], lnQi[:], -256.0, lnCr[:],
                                                        OP.add, OP.add), "sV")
                    v_q = C["sV"]

                    t_tr_base = C["sT"]
                    s_ev_base = C["sS"]

                    @blk.tensor
                    def _(tensor):
                        tensor.wait_ge(sV, v_q)
                        for half in range(2):
                            if half == 0:
                                wait_free(tensor, "pt")
                            else:
                                tensor.wait_ge(sS, s_ev_base + 1)
                            last = None
                            for k in range(4):
                                hc = half * 4 + k
                                last = tensor.transpose(pt[:, k * 128:(k + 1) * 128],
                                                        q_tok[:, hc * 128:(hc + 1) * 128],
                                                        identb[:])
                            inc(last, "sT")

                    @blk_scalar
                    def _(scalar):
                        for half in range(2):
                            scalar.wait_ge(sT, t_tr_base + half + 1)
                            hc0 = half * 4
                            inc(scalar.copy(
                                bass.AP(qT, hc0 * TB + j * 128, [[NC * TB, P], [TB, 4], [1, 128]]),
                                bass.AP(pt, 0, [[512, P], [128, 4], [1, 128]])), "sS")
                    free["pt"] = ("sS", C["sS"])

                v_lnq_done = C["sV"]
                s_qT_done = C["sS"]

                gw_tk = []
                t_gu_pred = C["sT"]

                @blk.sync
                def _(sync):
                    for g in range(16):
                        ic0 = 2 * g
                        if ic0 - 8 >= 0:
                            sync.wait_ge(sT, t_gu_pred + 2 * (ic0 - 8 + 2))
                        elif t_gu_hist:
                            sync.wait_ge(sT, t_gu_hist[-1] + 2 * (24 + ic0 + 2))
                        slot = ic0 % 8
                        fns = []
                        for wsb, dscr in ((wgr, dswg), (wur, dswu)):
                            fns.append(lambda wsb=wsb, dscr=dscr, slot=slot, ic0=ic0:
                                       sync.dma_start(
                                           bass.AP(wsb, slot * 1024,
                                                   [[8 * 1024, P], [P, 16], [1, P]]),
                                           bass.AP(dscr, ic0 * NC * P * P,
                                                   [[P, P], [P * P, 16], [1, P]])))
                        gw_tk.append(QA.issue(sync, fns))

                t_gu_base = C["sT"]
                assert t_gu_base == t_gu_pred
                t_gu_hist.append(t_gu_base)

                @blk.tensor
                def _(tensor):
                    tensor.wait_ge(sS, s_qT_done)
                    tensor.wait_ge(sV, v_lnq_done)
                    for ic in range(NI):
                        pg = pg0 if ic % 2 == 0 else pg1
                        pu = pu0 if ic % 2 == 0 else pu1
                        QA.wait(tensor, gw_tk[ic // 2])
                        if ic >= 2:
                            tensor.wait_ge(sS, s_qT_done + 2 * (ic - 1))
                            tensor.wait_ge(sV, v_lnq_done + (ic - 1))
                        slot = ic % 8
                        last = None
                        for hc in range(NC):
                            last = tensor.matmul(
                                pg[:], wgr[:, slot * 1024 + hc * 128:slot * 1024 + (hc + 1) * 128],
                                qT[:, hc * TB:(hc + 1) * TB],
                                start=(hc == 0), stop=(hc == NC - 1))
                        inc(last, "sT")
                        for hc in range(NC):
                            last = tensor.matmul(
                                pu[:], wur[:, slot * 1024 + hc * 128:slot * 1024 + (hc + 1) * 128],
                                qT[:, hc * TB:(hc + 1) * TB],
                                start=(hc == 0), stop=(hc == NC - 1))
                        inc(last, "sT")

                @blk_scalar
                def _(scalar):
                    for ic in range(NI):
                        pg = pg0 if ic % 2 == 0 else pg1
                        scalar.wait_ge(sT, t_gu_base + 2 * ic + 1)
                        scalar.activation(Rsb[:], pg[:], AF.Relu)
                        inc(scalar.activation(R2sb[:], Rsb[:], AF.Square), "sS", 2)

                @blk_vector
                def _(vector):
                    for ic in range(NI):
                        pu = pu0 if ic % 2 == 0 else pu1
                        vector.wait_ge(sT, t_gu_base + 2 * ic + 2)
                        vector.wait_ge(sS, s_qT_done + 2 * (ic + 1))
                        vector.tensor_mul(harena[:, ic * TB:(ic + 1) * TB], R2sb[:], pu[:])
                        hsl = harena[:, ic * TB:(ic + 1) * TB]
                        if ic == 0:
                            vector.tensor_tensor(maxrun[:], hsl, hsl, OP.max)
                            inc(vector.tensor_tensor(minrun[:], hsl, hsl, OP.min), "sV")
                        else:
                            vector.tensor_tensor(maxrun[:], hsl, maxrun[:], OP.max)
                            inc(vector.tensor_tensor(minrun[:], hsl, minrun[:], OP.min), "sV")
                v_B_done = C["sV"]

                @blk_vector
                def _(vector):
                    inc(vector.scalar_tensor_tensor(maxrun[:], minrun[:], -1.0, maxrun[:],
                                                    OP.mult, OP.max), "sV")
                v_B_done = C["sV"]

                @blk.tensor
                def _(tensor):
                    tensor.wait_ge(sV, v_B_done)
                    wait_free(tensor, "pt")
                    last = None
                    for k in range(4):
                        last = tensor.transpose(pt[:, k * 128:(k + 1) * 128],
                                                maxrun[:, k * 128:(k + 1) * 128], identb[:])
                    inc(last, "sT")
                t_mx = C["sT"]

                @blk_vector
                def _(vector):
                    vector.wait_ge(sT, t_mx)
                    for k in range(4):
                        vector.reduce_max(lnS[:, 12:13], pt[:, k * 128:(k + 1) * 128], AX.X)
                        vector.tensor_mul(lnS[:, 13:14], acol[:, k:k + 1], acol[:, k:k + 1])
                        vector.tensor_mul(lnS[:, 13:14], lnS[:, 13:14], acol[:, k:k + 1])
                        vector.tensor_scalar(lnS[:, 13:14], lnS[:, 13:14], k1bc[:, 0:1],
                                             None, OP.mult)
                        vector.tensor_mul(lnS[:, 14:15], lnS[:, 13:14], lnS[:, 12:13])
                        vector.tensor_scalar(lnS[:, 14:15], lnS[:, 14:15], EPS, None, OP.max)
                        vector.tensor_scalar(ocols[:, k:k + 1], lnS[:, 14:15], k2bc[:, 0:1],
                                             None, OP.mult)
                        vector.reciprocal(lnS[:, 15:16], lnS[:, 14:15])
                        vector.tensor_mul(lnS[:, 0:1], lnS[:, 14:15], lnS[:, 15:16])
                        vector.tensor_scalar(lnS[:, 0:1], lnS[:, 0:1], -1.0, 2.0,
                                             OP.mult, OP.add)
                        vector.tensor_mul(lnS[:, 15:16], lnS[:, 15:16], lnS[:, 0:1])
                        vector.tensor_scalar(lnS[:, 15:16], lnS[:, 15:16], 127.0, None, OP.mult)
                        vector.tensor_mul(lnS[:, 12:13], lnS[:, 13:14], lnS[:, 15:16])
                        vector.tensor_copy(acol[:, k:k + 1], lnS[:, 12:13])
                    inc(vector.engine_nop(), "sV")
                free["pt"] = ("sV", C["sV"])
                v_scol = C["sV"]

                @blk.tensor
                def _(tensor):
                    tensor.wait_ge(sV, v_scol)
                    wait_free(tensor, "psmall")
                    last = None
                    for k in range(4):
                        last = tensor.matmul(psmall[0:1, k * 128:(k + 1) * 128],
                                             acol[:, k:k + 1], identf[:], start=True, stop=True)
                    inc(last, "sT")
                t_row = C["sT"]

                @blk_vector
                def _(vector):
                    vector.wait_ge(sT, t_row)
                    inc(vector.tensor_copy(sbrow[0:1, :], psmall[0:1, 0:TB]), "sV")
                free["psmall"] = ("sV", C["sV"])
                v_row = C["sV"]

                @blk.tensor
                def _(tensor):
                    tensor.wait_ge(sV, v_row)
                    inc(tensor.matmul(pd0[:], onesr[0:1, :], sbrow[0:1, :],
                                      start=True, stop=True), "sT")
                t_srep = C["sT"]

                @blk_vector
                def _(vector):
                    vector.wait_ge(sT, t_srep)
                    last = None
                    for ic in range(NI):
                        vector.tensor_tensor(m1sb[:], harena[:, ic * TB:(ic + 1) * TB],
                                             pd0[:], OP.mult)
                        vector.tensor_scalar(lnQi[:, 0:TB], m1sb[:], 256.0, None, OP.add)
                        vector.scalar_tensor_tensor(lnCr[:, 0:TB], lnQi[:, 0:TB], -255.5,
                                                    m1sb[:], OP.add, OP.is_le)
                        last = vector.scalar_tensor_tensor(
                            harena[:, ic * TB:(ic + 1) * TB], lnQi[:, 0:TB], -256.0,
                            lnCr[:, 0:TB], OP.add, OP.add)
                    inc(last, "sV")
                v_ih = C["sV"]

                for sp in range(2):
                    wd_tk = []
                    t_dn_pred = C["sT"]

                    @blk.sync
                    def _(sync):
                        for icd in range(NI):
                            if icd - 8 >= 0:
                                sync.wait_ge(sT, t_dn_pred + icd - 8 + 1)
                            elif t_dn_hist:
                                sync.wait_ge(sT, t_dn_hist[-1] + 24 + icd + 1)
                            slot = icd % 8
                            wd_tk.append(QA.issue(sync, [
                                lambda icd=icd, slot=slot: sync.dma_start(
                                    bass.AP(wdr, slot * 1024,
                                            [[8 * 1024, P], [128, NC], [1, P]]),
                                    bass.AP(dswd, icd * P * P,
                                            [[P, P], [NI * P * P, NC], [1, P]]))]))

                    t_dn_base = C["sT"]
                    assert t_dn_base == t_dn_pred
                    t_dn_hist.append(t_dn_base)
                    v_prev_evac = C["sV"]

                    @blk.tensor
                    def _(tensor):
                        tensor.wait_ge(sV, v_prev_evac)
                        for ic in range(NI):
                            QA.wait(tensor, wd_tk[ic])
                            last = None
                            for tc2 in range(2):
                                tc = 2 * sp + tc2
                                for half in range(2):
                                    pp = (pg0, pg1, pu0, pu1)[tc2 * 2 + half]
                                    last = tensor.matmul(
                                        pp[:],
                                        harena[:, ic * TB + tc * 128:ic * TB + (tc + 1) * 128],
                                        wdr[:, (ic % 8) * 1024 + half * 512:(ic % 8) * 1024 + (half + 1) * 512],
                                        start=(ic == 0), stop=(ic == NI - 1))
                            inc(last, "sT")
                    t_dn_done = C["sT"]

                    @blk_vector
                    def _(vector):
                        vector.wait_ge(sT, t_dn_done)
                        last = None
                        for tc2 in range(2):
                            tc = 2 * sp + tc2
                            tile = 4 * b + tc
                            for half in range(2):
                                pp = (pg0, pg1, pu0, pu1)[tc2 * 2 + half]
                                xs2 = x_sb[:, tile * DM + half * 512:tile * DM + (half + 1) * 512]
                                last = vector.scalar_tensor_tensor(
                                    xs2, pp[:], ocols[:, tc:tc + 1], xs2, OP.mult, OP.add)
                        inc(last, "sV")

    def emit_attn():
        v_ffn_done = C["sV"]
        with (
            nc.sbuf_tensor("anorm", [P, NPAIR * S], BF16) as anorm,
            nc.sbuf_tensor("bvrep", [P, DM], F32) as bvrep,
            nc.sbuf_tensor("borep", [P, DM], F32) as borep,
        ):
            with (
                nc.sbuf_tensor("y2T", [P, NC * S], BF16) as y2T,
                nc.sbuf_tensor("vaug", [P, NT * 1040], BF16) as vaug,
            ):

                for tile in range(NT):
                    xsl = x_sb[:, tile * DM:(tile + 1) * DM]

                    @blk_vector
                    def _(vector):
                        if tile == 0:
                            vector.wait_ge(sV, v_ffn_done)
                        vector.reduce_sum(lnS[:, 0:1], xsl, AX.X)
                        inc(vector.tensor_scalar(lnS[:, 1:2], lnS[:, 0:1], -1.0 / DM,
                                                 None, OP.mult), "sV")
                    v_m = C["sV"]

                    @blk_scalar
                    def _(scalar):
                        scalar.wait_ge(sV, v_m)
                        inc(scalar.activation(lnA[:], xsl, AF.Square, bias=lnS[:, 1:2],
                                              accum_out=lnS[:, 2:3]), "sS")
                    s_sq = C["sS"]

                    @blk_vector
                    def _(vector):
                        vector.wait_ge(sS, s_sq)
                        inc(vector.tensor_scalar(lnS[:, 3:4], lnS[:, 2:3], 1.0 / DM, EPS,
                                                 OP.mult, OP.add), "sV")
                    v_t0 = C["sV"]

                    @blk_scalar
                    def _(scalar):
                        scalar.wait_ge(sV, v_t0)
                        inc(scalar.activation(lnS[:, 4:5], lnS[:, 3:4], AF.Sqrt), "sS")
                    s_rt = C["sS"]

                    @blk_vector
                    def _(vector):
                        vector.wait_ge(sS, s_rt)
                        vector.reciprocal(lnS[:, 5:6], lnS[:, 4:5])
                        for _it in range(2):
                            vector.tensor_mul(lnS[:, 6:7], lnS[:, 5:6], lnS[:, 5:6])
                            vector.tensor_mul(lnS[:, 6:7], lnS[:, 6:7], lnS[:, 3:4])
                            vector.tensor_scalar(lnS[:, 6:7], lnS[:, 6:7], -0.5, 1.5,
                                                 OP.mult, OP.add)
                            vector.tensor_mul(lnS[:, 5:6], lnS[:, 5:6], lnS[:, 6:7])
                        vector.tensor_mul(lnS[:, 7:8], lnS[:, 1:2], lnS[:, 5:6])
                        vector.wait_ge(sT, C["sT"])
                        inc(vector.tensor_scalar(q_tok[:], xsl, lnS[:, 5:6], lnS[:, 7:8],
                                                 OP.mult, OP.add), "sV")
                    v_y2 = C["sV"]

                    t_tr_base = C["sT"]
                    s_ev_base = C["sS"]

                    @blk.tensor
                    def _(tensor):
                        tensor.wait_ge(sV, v_y2)
                        for half in range(2):
                            if half == 0:
                                wait_free(tensor, "pt")
                            else:
                                tensor.wait_ge(sS, s_ev_base + 1)
                            last = None
                            for k in range(4):
                                hc = half * 4 + k
                                last = tensor.transpose(pt[:, k * 128:(k + 1) * 128],
                                                        q_tok[:, hc * 128:(hc + 1) * 128],
                                                        identb[:])
                            inc(last, "sT")

                    @blk_scalar
                    def _(scalar):
                        for half in range(2):
                            scalar.wait_ge(sT, t_tr_base + half + 1)
                            hc0 = half * 4
                            inc(scalar.copy(
                                bass.AP(y2T, hc0 * S + tile * 128,
                                        [[NC * S, P], [S, 4], [1, 128]]),
                                bass.AP(pt, 0, [[512, P], [128, 4], [1, 128]])), "sS")
                    free["pt"] = ("sS", C["sS"])

                s_y2T_done = C["sS"]

                with (
                    nc.sbuf_tensor("wslab_v", [P, NC * DM], BF16) as wslab,
                    nc.sbuf_tensor("brow", [1, DM], F32) as brow,
                ):
                    tk_wv = [None]

                    @blk.gpsimd
                    def _(g):
                        g.wait_ge(sV, v_ffn_done)
                        tk_wv[0] = QG.issue(g, [
                            lambda: g.dma_start(
                                bass.AP(wslab, 0, [[NC * DM, P], [DM, NC], [1, DM]]),
                                bass.AP(dWv, 0, [[DM, P], [P * DM, NC], [1, DM]])),
                            lambda: g.dma_start(brow[0:1, 0:DM], dbv.ap())])

                    for h in range(4):
                        if h == 2:
                            t_snap = C["sT"]

                            @blk.gpsimd
                            def _(g):
                                g.wait_ge(sT, t_snap)
                                tk_wv[0] = QG.issue(g, [
                                    lambda: g.dma_start(brow[0:1, 0:DM], dbo.ap())])

                        @blk.tensor
                        def _(tensor):
                            QG.wait(tensor, tk_wv[0])
                            wait_free(tensor, "psmall")
                            inc(tensor.matmul(psmall[:, 0:512], onesr[0:1, :],
                                              brow[0:1, (h % 2) * 512:(h % 2 + 1) * 512],
                                              start=True, stop=True), "sT")
                        t_h = C["sT"]

                        @blk_vector
                        def _(vector):
                            vector.wait_ge(sT, t_h)
                            dst = bvrep if h < 2 else borep
                            inc(vector.tensor_copy(dst[:, (h % 2) * 512:(h % 2 + 1) * 512],
                                                   psmall[:, 0:512]), "sV")
                        free["psmall"] = ("sV", C["sV"])

                    @blk_vector
                    def _(vector):
                        inc(vector.memset(vaug[:], 1.0), "sV")
                    v_ones = C["sV"]

                    t_v_base = C["sT"]

                    @blk.tensor
                    def _(tensor):
                        tensor.wait_ge(sS, s_y2T_done)
                        QG.wait(tensor, tk_wv[0])
                        tensor.wait_ge(sV, v_ones)
                        for tc in range(NT):
                            for half in range(2):
                                psv = pg0 if (2 * tc + half) % 2 == 0 else pg1
                                idx = 2 * tc + half
                                if idx >= 2:
                                    tensor.wait_ge(sV, v_ones + idx - 1)
                                last = None
                                for dc in range(NC):
                                    last = tensor.matmul(
                                        psv[:], y2T[:, dc * S + tc * 128:dc * S + (tc + 1) * 128],
                                        wslab[:, dc * DM + half * 512:dc * DM + (half + 1) * 512],
                                        start=(dc == 0), stop=(dc == NC - 1))
                                inc(last, "sT")

                    @blk_vector
                    def _(vector):
                        for tc in range(NT):
                            for half in range(2):
                                psv = pg0 if (2 * tc + half) % 2 == 0 else pg1
                                idx = 2 * tc + half
                                vector.wait_ge(sT, t_v_base + idx + 1)
                                inc(vector.tensor_add(
                                    bass.AP(vaug, tc * 1040 + half * 520,
                                            [[NT * 1040, P], [65, 8], [1, 64]]),
                                    bass.AP(bvrep, half * 512, [[DM, P], [64, 8], [1, 64]]),
                                    psv[:]), "sV")
                    v_vaug_done = C["sV"]

                with (
                    nc.sbuf_tensor("wqp", [P, NC * 128], BF16) as wqp,
                    nc.sbuf_tensor("wkp", [P, NC * 128], BF16) as wkp,
                    nc.sbuf_tensor("Qp", [P, S], BF16) as Qp,
                    nc.sbuf_tensor("Kp", [P, S], BF16) as Kp,
                    nc.sbuf_tensor("Ebuf", [P, 2048], BF16) as Ebuf,
                    nc.sbuf_tensor("zi", [P, 1024], BF16) as zi,
                    nc.sbuf_tensor("zrA", [P, 512], BF16) as zrA,
                    nc.sbuf_tensor("zrB", [P, 512], BF16) as zrB,
                    nc.sbuf_tensor("nbs", [P, 1024], BF16) as nbs,
                ):
                    nb_dma_hist = []
                    for c in range(NPAIR):
                        tk_qk = [None]

                        @blk.gpsimd
                        def _(g):
                            if c == 0:
                                g.wait_ge(sV, v_vaug_done)
                            else:
                                g.wait_ge(sV, C["sV"])
                            tk_qk[0] = QG.issue(g, [
                                lambda: g.dma_start(
                                    bass.AP(wqp, 0, [[NC * 128, P], [128, NC], [1, 128]]),
                                    bass.AP(dWq, 128 * c, [[DM, P], [P * DM, NC], [1, 128]])),
                                lambda: g.dma_start(
                                    bass.AP(wkp, 0, [[NC * 128, P], [128, NC], [1, 128]]),
                                    bass.AP(dWk, 128 * c, [[DM, P], [P * DM, NC], [1, 128]]))])

                        t_qk_base = C["sT"]
                        s_qk_pre = C["sS"]

                        @blk.tensor
                        def _(tensor):
                            QG.wait(tensor, tk_qk[0])
                            tensor.wait_ge(sV, C["sV"] if c > 0 else v_vaug_done)
                            for s4 in range(4):
                                if s4 >= 1:
                                    tensor.wait_ge(sS, s_qk_pre + 2 * s4)
                                last = None
                                for dc in range(NC):
                                    last = tensor.matmul(
                                        pg0[:], wqp[:, dc * 128:(dc + 1) * 128],
                                        y2T[:, dc * S + s4 * 512:dc * S + (s4 + 1) * 512],
                                        start=(dc == 0), stop=(dc == NC - 1))
                                inc(last, "sT")
                                for dc in range(NC):
                                    last = tensor.matmul(
                                        pg1[:], wkp[:, dc * 128:(dc + 1) * 128],
                                        y2T[:, dc * S + s4 * 512:dc * S + (s4 + 1) * 512],
                                        start=(dc == 0), stop=(dc == NC - 1))
                                inc(last, "sT")

                        @blk_scalar
                        def _(scalar):
                            for s4 in range(4):
                                scalar.wait_ge(sT, t_qk_base + 2 * s4 + 1)
                                scalar.activation(Qp[:, s4 * 512:(s4 + 1) * 512], pg0[:],
                                                  AF.Identity, bias=bcols[:, c:c + 1],
                                                  scale=f32(1.0))
                                scalar.wait_ge(sT, t_qk_base + 2 * s4 + 2)
                                inc(scalar.activation(Kp[:, s4 * 512:(s4 + 1) * 512], pg1[:],
                                                      AF.Identity, bias=bcols[:, NC + c:NC + c + 1],
                                                      scale=f32(1.0)), "sS", 2)
                        s_qk = C["sS"]

                        for s4 in range(4):
                            kcs = list(range(4 * s4 + 4))
                            nkc = len(kcs)
                            t_core_base = C["sT"]
                            s_core_base = C["sS"]
                            v_core_base = C["sV"]

                            @blk.tensor
                            def _(tensor):
                                tensor.wait_ge(sS, s_qk)
                                for i4, kc in enumerate(kcs):
                                    o = max(0, 128 * kc - 512 * s4)
                                    if i4 >= 1:
                                        tensor.wait_ge(sS, s_core_base + 2 * i4)
                                    tensor.matmul(pg0[:, o:512],
                                                  Kp[0:64, kc * 128:(kc + 1) * 128],
                                                  Qp[0:64, s4 * 512 + o:(s4 + 1) * 512],
                                                  start=True, stop=True)
                                    inc(tensor.matmul(pg1[:, o:512],
                                                      Kp[64:128, kc * 128:(kc + 1) * 128],
                                                      Qp[64:128, s4 * 512 + o:(s4 + 1) * 512],
                                                      start=True, stop=True), "sT")
                                    tensor.wait_ge(sV, v_core_base + i4 + 1)
                                    eb = (i4 % 2) * 1024
                                    tensor.matmul(pu0[0:65, o:512],
                                                  bass.AP(vaug, kc * 1040 + 65 * (2 * c),
                                                          [[NT * 1040, P], [1, 65]]),
                                                  Ebuf[:, eb + o:eb + 512],
                                                  start=(i4 == 0), stop=(i4 == nkc - 1))
                                    inc(tensor.matmul(pu1[0:65, o:512],
                                                      bass.AP(vaug, kc * 1040 + 65 * (2 * c + 1),
                                                              [[NT * 1040, P], [1, 65]]),
                                                      Ebuf[:, eb + 512 + o:eb + 1024],
                                                      start=(i4 == 0), stop=(i4 == nkc - 1)),
                                        "sT")
                            t_pv_done = C["sT"]

                            @blk_scalar
                            def _(scalar):
                                for i4, kc in enumerate(kcs):
                                    o = max(0, 128 * kc - 512 * s4)
                                    eb = (i4 % 2) * 1024
                                    scalar.wait_ge(sT, t_core_base + 2 * i4 + 1)
                                    if i4 >= 2:
                                        scalar.wait_ge(sT, t_core_base + 2 * (i4 - 1))
                                    scalar.activation(Ebuf[:, eb + o:eb + 512], pg0[:, o:512],
                                                      AF.Exp, scale=f32(0.125), bias=f32(0.0))
                                    inc(scalar.activation(Ebuf[:, eb + 512 + o:eb + 1024],
                                                          pg1[:, o:512], AF.Exp,
                                                          scale=f32(0.125), bias=f32(0.0)),
                                        "sS", 2)

                            @blk_vector
                            def _(vector):
                                for i4, kc in enumerate(kcs):
                                    o = max(0, 128 * kc - 512 * s4)
                                    eb = (i4 % 2) * 1024
                                    vector.wait_ge(sS, s_core_base + 2 * (i4 + 1))
                                    if kc >= 4 * s4:
                                        vector.tensor_mul(Ebuf[:, eb + o:eb + o + 128],
                                                          Ebuf[:, eb + o:eb + o + 128], tri[:])
                                        vector.tensor_mul(
                                            Ebuf[:, eb + 512 + o:eb + 512 + o + 128],
                                            Ebuf[:, eb + 512 + o:eb + 512 + o + 128], tri[:])
                                    inc(vector.engine_nop(), "sV")

                            @blk_vector
                            def _(vector):
                                vector.wait_ge(sT, t_pv_done)
                                with nc.allow_low_precision(reason="1/Z as bf16 matmul operand"):
                                    vector.reciprocal(zi[64:65, 0:512], pu0[64:65, :])
                                    inc(vector.reciprocal(zi[64:65, 512:1024], pu1[64:65, :]),
                                        "sV")
                            v_zi = C["sV"]

                            @blk.tensor
                            def _(tensor):
                                tensor.wait_ge(sV, v_zi)
                                tensor.matmul(pd0[0:64, :], onesbb[64:65, :],
                                              zi[64:65, 0:512], start=True, stop=True)
                                inc(tensor.matmul(pd1[0:64, :], onesbb[64:65, :],
                                                  zi[64:65, 512:1024], start=True, stop=True),
                                    "sT")
                            t_z = C["sT"]

                            nbh = (4 * c + s4) % 2

                            @blk_scalar
                            def _(scalar):
                                scalar.wait_ge(sT, t_z)
                                scalar.copy(zrA[0:64, :], pd0[0:64, :])
                                inc(scalar.copy(zrB[0:64, :], pd1[0:64, :]), "sS")
                            s_zr = C["sS"]

                            @blk_vector
                            def _(vector):
                                vector.wait_ge(sS, s_zr)
                                vector.tensor_mul(
                                    anorm[0:64, c * S + s4 * 512:c * S + (s4 + 1) * 512],
                                    pu0[0:64, :], zrA[0:64, :])
                                if len(nb_dma_hist) >= 2:
                                    QG.wait(vector, nb_dma_hist[-2])
                                inc(vector.tensor_mul(nbs[0:64, nbh * 512:(nbh + 1) * 512],
                                                      pu1[0:64, :], zrB[0:64, :]), "sV")
                            v_nb = C["sV"]

                            @blk.gpsimd
                            def _(g):
                                g.wait_ge(sV, v_nb)
                                nb_dma_hist.append(QG.issue(g, [lambda: g.dma_start(
                                    anorm[64:128, c * S + s4 * 512:c * S + (s4 + 1) * 512],
                                    nbs[0:64, nbh * 512:(nbh + 1) * 512])]))

            v_pair_done = C["sV"]
            with nc.sbuf_tensor("wslab_o", [P, NC * DM], BF16) as wslab_o:
                tk_wo = [None]

                @blk.gpsimd
                def _(g):
                    g.wait_ge(sV, v_pair_done)
                    tk_wo[0] = QG.issue(g, [lambda: g.dma_start(
                        bass.AP(wslab_o, 0, [[NC * DM, P], [DM, NC], [1, DM]]),
                        bass.AP(dWo, 0, [[DM, P], [P * DM, NC], [1, DM]]))])

                t_op_base = C["sT"]

                @blk.tensor
                def _(tensor):
                    QG.wait_all(tensor)
                    tensor.wait_ge(sV, v_pair_done)
                    for tc in range(NT):
                        for half in range(2):
                            pdx = pd0 if (2 * tc + half) % 2 == 0 else pd1
                            idx = 2 * tc + half
                            if idx >= 2:
                                tensor.wait_ge(sV, v_pair_done + idx - 1)
                            last = None
                            for cc in range(NPAIR):
                                last = tensor.matmul(
                                    pdx[:], anorm[:, cc * S + tc * 128:cc * S + (tc + 1) * 128],
                                    wslab_o[:, cc * DM + half * 512:cc * DM + (half + 1) * 512],
                                    start=(cc == 0), stop=(cc == NPAIR - 1))
                            inc(last, "sT")

                @blk_vector
                def _(vector):
                    for tc in range(NT):
                        for half in range(2):
                            pdx = pd0 if (2 * tc + half) % 2 == 0 else pd1
                            idx = 2 * tc + half
                            vector.wait_ge(sT, t_op_base + idx + 1)
                            xs2 = x_sb[:, tc * DM + half * 512:tc * DM + (half + 1) * 512]
                            vector.tensor_add(m1sb[:], pdx[:],
                                              borep[:, half * 512:(half + 1) * 512])
                            inc(vector.tensor_add(xs2, xs2, m1sb[:]), "sV")

    emit_ffn(0)
    emit_attn()
    emit_ffn(1)

    v_all = C["sV"]

    @blk.sync
    def _(sync):
        sync.wait_ge(sV, v_all)
        tko = QA.issue(sync, [lambda: sync.dma_start(
            bass.AP(dout, 0, [[DM, P], [P * DM, NT], [1, DM]]),
            bass.AP(x_sb, 0, [[NT * DM, P], [DM, NT], [1, DM]]))])
        QA.wait(sync, tko)
        QG.wait_all(sync)

    es.close()
    return nc


_NC_CACHE = None


def kernel(L0_attn_x, L0_attn_Wq, L0_attn_bq, L0_attn_Wk, L0_attn_bk,
           L0_attn_Wv, L0_attn_bv, L0_attn_Wo, L0_attn_bo,
           L0_ffn_w_gate, L0_ffn_w_up, L0_ffn_w_down):
    global _NC_CACHE
    if _NC_CACHE is None:
        _NC_CACHE = build_core()
    nc = _NC_CACHE

    x = np.ascontiguousarray(np.asarray(L0_attn_x, dtype=np.float32))
    shared = dict(
        Wq=np.ascontiguousarray(np.asarray(L0_attn_Wq, dtype=np.float32)),
        Wk=np.ascontiguousarray(np.asarray(L0_attn_Wk, dtype=np.float32)),
        Wv=np.ascontiguousarray(np.asarray(L0_attn_Wv, dtype=np.float32)),
        Wo=np.ascontiguousarray(np.asarray(L0_attn_Wo, dtype=np.float32)),
        bq=np.ascontiguousarray(np.asarray(L0_attn_bq, dtype=np.float32).reshape(1, DM)),
        bk=np.ascontiguousarray(np.asarray(L0_attn_bk, dtype=np.float32).reshape(1, DM)),
        bv=np.ascontiguousarray(np.asarray(L0_attn_bv, dtype=np.float32).reshape(1, DM)),
        bo=np.ascontiguousarray(np.asarray(L0_attn_bo, dtype=np.float32).reshape(1, DM)),
        wg=np.ascontiguousarray(np.asarray(L0_ffn_w_gate, dtype=np.float32)),
        wu=np.ascontiguousarray(np.asarray(L0_ffn_w_up, dtype=np.float32)),
        wd=np.ascontiguousarray(np.asarray(L0_ffn_w_down, dtype=np.float32)),
        idb=np.eye(P, dtype=ml_dtypes.bfloat16),
        idf=np.eye(P, dtype=np.float32),
        tri=np.triu(np.ones((P, P), dtype=ml_dtypes.bfloat16)),
        onesf=np.ones((1, P), dtype=np.float32),
        onesb=np.ones((1, P), dtype=ml_dtypes.bfloat16),
    )
    in_maps = [dict(x=np.ascontiguousarray(x[i]), **shared) for i in range(8)]
    res = run_bass_kernel_spmd(nc, in_maps, core_ids=list(range(8)))
    out = np.stack([np.asarray(r["out"], dtype=np.float32) for r in res.results], axis=0)
    return out
